# version 24
# baseline (speedup 1.0000x reference)
import numpy as np
import concourse.bass as bass
import concourse.mybir as mybir
from concourse.alu_op_type import AluOpType
from concourse.bass_utils import run_bass_kernel_spmd
from concourse.hw_specs import get_activation_tables

F32 = mybir.dt.float32
BF16 = mybir.dt.bfloat16
U32 = mybir.dt.uint32
AF = mybir.ActivationFunctionType
OP = AluOpType

NCORES = 8
NTOK = 2048
T = 512
NCH = NTOK // T
G = 32
NSB = T // 16
NH = 16
EPS = 1e-5
RSQ = float(1.0 / np.sqrt(128.0))


class Sem:
    def __init__(self, nc, name):
        self.h = nc.semaphore(name).__enter__()
        self.n = 0

    def bump(self, amt):
        self.n += amt
        return self.n


class Op:
    __slots__ = ("waits", "fn", "incs", "drain")

    def __init__(self, waits, fn, incs, drain):
        self.waits, self.fn, self.incs, self.drain = waits, fn, incs, drain


def kernel(L0_attn_hidden_states, L0_attn_wq, L0_attn_bq, L0_attn_wk, L0_attn_bk,
           L0_attn_wv, L0_attn_bv, L0_attn_wl, L0_attn_bl,
           L0_ffn_up, L0_ffn_down, L0_ffn_perm):
    import ml_dtypes
    x = np.ascontiguousarray(np.asarray(L0_attn_hidden_states, dtype=np.float32)
                             .reshape(NCORES * NTOK, 4096))

    def blockT(w):
        w4 = np.asarray(w, np.float32).reshape(G, 128, G, 128)
        return np.ascontiguousarray(w4.transpose(0, 3, 2, 1).reshape(G, 128, 4096))

    wT = np.ascontiguousarray(np.stack(
        [blockT(L0_attn_wq), blockT(L0_attn_wk), blockT(L0_attn_wv)]))
    bias = np.ascontiguousarray(
        np.stack([np.ascontiguousarray(np.asarray(b, np.float32).reshape(G, 128).T)
                  for b in (L0_attn_bq, L0_attn_bk, L0_attn_bv)]))
    up5 = np.asarray(L0_ffn_up, np.float32).reshape(NH, 2, 128, 1024) \
        .transpose(0, 2, 1, 3).reshape(NH, 128, 2048)
    dn5 = np.asarray(L0_ffn_down, np.float32).reshape(NH, 8, 128, 256) \
        .transpose(0, 2, 1, 3).reshape(NH, 128, 2048)
    updn = np.ascontiguousarray(np.concatenate([up5, dn5], axis=2))
    permT = np.ascontiguousarray(
        np.asarray(L0_ffn_perm).astype(np.int64).astype(np.uint32)
        .reshape(G, 128).T)
    id32 = np.eye(128, dtype=np.float32)
    idbf = np.eye(128, dtype=ml_dtypes.bfloat16)

    nc = build_program()

    in_maps = []
    for i in range(NCORES):
        xT = np.ascontiguousarray(x[i * NTOK:(i + 1) * NTOK].T)
        in_maps.append({"x0T": xT, "wT": wT, "bias": bias, "updn": updn,
                        "permT": permT, "id32": id32, "idbf": idbf})
    res = run_bass_kernel_spmd(nc, in_maps, core_ids=list(range(NCORES)))
    out = np.concatenate([r["out"] for r in res.results], axis=0)
    return out.reshape(2, 8192, 4096)


def build_program():
    nc = bass.Bass("TRN2", target_bir_lowering=False)
    OPL = {k: [] for k in ("sync", "tensor", "scalar", "vector", "gpsimd")}

    d_x0T = nc.dram_tensor("x0T", [4096, NTOK], F32, kind="ExternalInput")
    d_wT = nc.dram_tensor("wT", [3, G, 128, 4096], F32, kind="ExternalInput")
    d_bias = nc.dram_tensor("bias", [3, 128, G], F32, kind="ExternalInput")
    d_updn = nc.dram_tensor("updn", [NH, 128, 4096], F32, kind="ExternalInput")
    d_perm = nc.dram_tensor("permT", [128, G], U32, kind="ExternalInput")
    d_id32 = nc.dram_tensor("id32", [128, 128], F32, kind="ExternalInput")
    d_idbf = nc.dram_tensor("idbf", [128, 128], BF16, kind="ExternalInput")
    d_out = nc.dram_tensor("out", [NTOK, 4096], F32, kind="ExternalOutput")
    d_wst = nc.dram_tensor("wstash", [3, G, 128, 4096], BF16, kind="Internal")
    d_ust = nc.dram_tensor("updnstash", [NH, 128, 4096], BF16, kind="Internal")
    d_x1bf = nc.dram_tensor("x1bf", [4096, T], BF16, kind="Internal")
    d_ffnat = nc.dram_tensor("ffnat", [4096, T], F32, kind="Internal")

    arena_items = []
    arena_size = [0]

    def sb_at(off, name, shape, dtype):
        import functools, operator
        per_part = functools.reduce(operator.mul, shape[1:], 1) * mybir.dt.size(dtype)
        assert off % 32 == 0, name
        arena_size[0] = max(arena_size[0], off + per_part)
        h = nc.virtual_sbuf_tensor(name, shape, dtype)
        arena_items.append((h, off))
        return h

    sumT = sb_at(0, "sumT", [128, G * T], F32)
    qT = sb_at(65536, "qT", [128, G * T], BF16)
    kT = sb_at(98304, "kT", [128, G * T], BF16)
    x1bf = sb_at(65536, "x1bfp", [128, G * T], BF16)
    xbf_scr = [sb_at(98304 + i * 1024, f"xbfs{i}", [128, T], BF16) for i in (0, 1)]
    sq_scr = [sb_at(100352 + i * 1024, f"sqs{i}", [128, T], BF16) for i in (0, 1)]
    x0Tbuf = [sb_at(102400 + i * 2048, f"x0Tb{i}", [128, T], F32)
              for i in range(8)]
    vT = sb_at(131072, "vT", [128, G * T], BF16)
    gelu_sb = [sb_at(131072 + i * 8192, f"gelu{i}", [128, 8 * T], BF16)
               for i in (0, 1)]
    xg = [sb_at(147456 + i * 2048, f"xg{i}", [128, 2 * T], BF16)
          for i in (0, 1)] + [sb_at(155648 + i * 2048, f"xg{2+i}", [128, 2 * T],
                                    BF16) for i in (0, 1)]
    ffn_st = [sb_at(151552 + i * 2048, f"ffnst{i}", [128, T], F32)
              for i in (0, 1)] + [sb_at(159744 + i * 2048, f"ffnst{2+i}",
                                        [128, T], F32) for i in (0, 1)]
    out_st = [sb_at(131072 + i * 16384, f"outst{i}", [128, 4096], F32)
              for i in (0, 1)]
    x0bf = sb_at(163840, "x0bf", [128, G * T], BF16)
    ubuf = [sb_at(163840 + i * 4096, f"ubuf{i}", [128, 2048], BF16)
            for i in (0, 1)]
    dbuf = [sb_at(172032 + i * 4096, f"dbuf{i}", [128, 2048], BF16)
            for i in (0, 1)]
    fbuf = [sb_at(172032 + i * 4096, f"fbuf{i}", [128, 2 * T], F32)
            for i in range(3)]
    probs_bd = [sb_at(184320 + i * 1024, f"probsbd{i}", [128, 4 * 128], BF16)
                for i in (0, 1)]
    vtok_sb = [sb_at(186368 + i * 1024, f"vtok{i}", [128, 4 * 128], BF16)
               for i in (0, 1)]
    rb_sb = [sb_at(188416, "rb0", [128, T], F32),
             sb_at(190464, "rb1", [128, T], F32)]
    onesK = sb_at(210816, "onesK", [128, 128], BF16)
    row_mean = sb_at(184320, "rowmean", [1, T], F32)
    row_var = sb_at(186368, "rowvar", [1, T], F32)
    row_A = sb_at(188416, "rowA", [1, T], F32)
    row_B = sb_at(190464, "rowB", [1, T], F32)
    row_hi = sb_at(192512, "rowhi", [1, T], BF16)
    row_lo = sb_at(193536, "rowlo", [1, T], BF16)
    wbuf = [sb_at(196608 + i * 4096, f"wbuf{i}", [128, 2048], BF16)
            for i in (0, 1, 2)]
    wbufx = [sb_at(i * 4096, f"wbufx{i}", [128, 2048], BF16) for i in range(3)]
    wbuf6 = wbuf + wbufx
    b_sb2 = sb_at(208896, "b_sb2", [128, 3 * G], F32)
    perm_sb = sb_at(209280, "perm_sb", [128, G], U32)
    id32_sb = sb_at(209408, "id32_sb", [128, 128], F32)
    idbf_sb = sb_at(209920, "idbf_sb", [128, 128], BF16)
    ones128 = sb_at(210176, "ones128", [128, 1], BF16)
    ones1w = sb_at(210208, "ones1w", [1, 128], BF16)
    s6_std = sb_at(210464, "s6_std", [128, 1], F32)
    s6_rstd = sb_at(210496, "s6_rstd", [128, 1], F32)
    s6_nmr = sb_at(210528, "s6_nmr", [128, 1], F32)
    s6_bn = sb_at(210560, "s6_bn", [128, 48], F32)
    s6_mv = sb_at(210752, "s6_mv", [128, 2], F32)
    eps_sb = sb_at(210784, "eps_sb", [128, 1], F32)
    assert arena_size[0] <= 212800, arena_size[0]

    backing = nc.sbuf_tensor("arena", [128, arena_size[0]],
                             mybir.dt.uint8).__enter__()
    base_addr = nc.lookup_mloc(backing).addr
    for h, off in arena_items:
        nc._tensor(h.name, h.shape, h.dtype, type="SB")
        m = nc.lookup_mloc(h)
        m.allocated = True
        m.addr = base_addr + off

    pb = [nc.psum_tensor(f"pb{i}", [128, 512], F32).__enter__() for i in range(8)]
    pv = []
    for i in (0, 1):
        mb = nc.lookup_mloc(pb[6 + i])
        h1 = nc.virtual_psum_tensor(f"pv{i}", [128, 512], BF16)
        nc._tensor(h1.name, h1.shape, h1.dtype, type="PSUM")
        m = nc.lookup_mloc(h1)
        m.allocated = True; m.addr = mb.addr + 0; m.bank = mb.bank
        pv.append(h1)
    ps = [pb[0], pb[1]]

    S = lambda n: Sem(nc, n)
    s_cst = S("cst")
    s_ld = [S(f"ld{i}") for i in range(8)]
    s_ldg = [S(f"ldg{i}") for i in range(8)]
    s_wB = [S("wB0"), S("wB1"), S("wB2")]
    s_wBg = [S(f"wBg{i}") for i in range(6)]
    s_stB = [S(f"stB{i}") for i in range(6)]
    s_wF = [S("wF0"), S("wF1")]
    s_wFg = [S("wFg0"), S("wFg1")]
    s_wD = [S("wD0"), S("wD1")]
    s_udc = S("udc")
    s_stF = [S("stF0"), S("stF1")]
    s_g = [S("g0"), S("g1"), S("g2"), S("g3")]
    s_xd = [S(f"xd{i}") for i in range(8)]
    s_fb = [S(f"fb{i}") for i in range(3)]
    s_fa = [S(f"fa{i}") for i in range(3)]
    s_sc = [S("sc0"), S("sc1"), S("sc2"), S("sc3")]
    s_out = [S("out0"), S("out1")]
    s_x1 = S("x1")
    s_peB, s_peBh, s_actB, s_actA = S("peB"), S("peBh"), S("actB"), S("actA")
    s_peCa, s_peCb = S("peCa"), S("peCb")
    s_actCa, s_dvev, s_dveCm = S("actCa"), S("dvev"), S("dveCm")
    s_dver, s_dvee = S("dver"), S("dvee")
    s_peD, s_peDb1, s_peDb2 = S("peD"), S("peDb1"), S("peDb2")
    s_actDs, s_dveDa, s_dveDc, s_dveDn, s_actDx = \
        S("actDs"), S("dveDa"), S("dveDc"), S("dveDn"), S("actDx")
    s_peU, s_actE, s_peDn, s_dveE = S("peU"), S("actE"), S("peDn"), S("dveE")
    s_dve6, s_pe6, s_dve6e, s_act6n = S("dve6"), S("pe6"), S("dve6e"), S("act6n")
    s_selfD, s_selfV, s_self6 = S("selfD"), S("selfV"), S("self6")

    tables = list(get_activation_tables(nc.m.arch).keys())
    TBL_LNEXP = tables.index("natural_log_exp_and_others")
    TBL_GELU = tables.index("gelu_and_others")
    TBL_SQRT = tables.index("sqrt_and_others")

    def load_table(idx):
        def f(e):
            return e.add_instruction(mybir.InstLoadActFuncSet(
                name=nc.get_next_instruction_name(), act_func_set_id=idx,
                ins=[], outs=[]))
        return f

    def op(engine, fn, waits=(), inc=None, amt=1, drain=False):
        incs = []
        if inc is not None:
            inc.bump(amt)
            incs.append((inc, amt, inc.n))
        OPL[engine].append(Op([(s, v) for (s, v) in waits if v > 0],
                              fn, incs, drain))

    op("sync", lambda e: e.dma_start(
        out=b_sb2.rearrange("p (t g) -> p t g", t=3),
        in_=d_bias.rearrange("t p g -> p t g")),
        inc=s_cst, amt=16)
    op("sync", lambda e: e.dma_start(out=perm_sb[:, :], in_=d_perm[:, :]),
       inc=s_cst, amt=16)
    op("sync", lambda e: e.dma_start(out=id32_sb[:, :], in_=d_id32[:, :]),
       inc=s_cst, amt=16)
    op("sync", lambda e: e.dma_start(out=idbf_sb[:, :], in_=d_idbf[:, :]),
       inc=s_cst, amt=16)
    op("vector", lambda e: e.memset(ones128[:, :], 1.0), inc=s_selfV)
    op("vector", lambda e: e.memset(ones1w[:, :], 1.0), inc=s_selfV)
    op("vector", lambda e: e.memset(onesK[:, :], 1.0), inc=s_selfV)
    op("vector", lambda e: e.memset(eps_sb[:, :], EPS), inc=s_selfV)
    cst_v = s_selfV.n
    op("scalar", load_table(TBL_LNEXP))

    def emit_a_loads(chx, queue):
        sems = s_ld if queue == "sync" else s_ldg
        base = [x.n for x in sems]
        for g in range(G):
            par = g % 8
            waits = []
            if g >= 8:
                waits.append((s_actA, 32 * chx + g - 7))
            if chx > 0 and g < 8:
                waits.append((s_peCb, chx * (NSB + 2)))
                waits.append((s_dveDa, chx * G))
            def f(e, g=g, par=par, chx=chx):
                return e.dma_start(out=x0Tbuf[par][:, :],
                                   in_=d_x0T[g * 128:(g + 1) * 128,
                                             chx * T:(chx + 1) * T])
            op(queue, f, waits=waits, inc=sems[par], amt=16)
        return (sems, base)

    def emit_a_casts(chx, lb, g0, g1):
        sems, lbase = lb
        for g in range(g0, g1):
            par = g % 8
            w2 = [(sems[par], lbase[par] + 16 * (g // 8 + 1))]
            if chx == 0 and g == 0:
                w2.append((s_cst, 64))
            if chx > 0:
                if g == 0:
                    w2.append((s_peU, chx * 128))
                elif g == 8:
                    w2.append((s_peDn, chx * 32))
                elif g == 20:
                    w2.append((s_peDb2, chx))
            if chx == 1 and g == 0:
                w2.append((s_udc, 16 * NH))
            def fc(e, g=g, par=par):
                return e.activation(out=x0bf[:, g * T:(g + 1) * T],
                                    in_=x0Tbuf[par][:, :], func=AF.Copy)
            op("scalar", fc, waits=w2, inc=s_actA)

    b_w_base_prev = None
    a_ld_base_next = None
    for ch in range(NCH):
        ch_tok = ch * T

        if ch == 0:
            a_ld_base = emit_a_loads(0, "sync")
        else:
            a_ld_base = a_ld_base_next
        a_cast_base = 32 * ch
        if ch == 0:
            emit_a_casts(0, a_ld_base, 0, G)

        b_pe_base = s_peB.n
        b_peh_base = s_peBh.n
        b_act_base = s_actB.n
        s_wBc = s_wBg if ch == 0 else s_wB
        wb_n = 6 if ch == 0 else 3
        wb = wbuf6 if ch == 0 else wbuf
        b_w_base = ([s_wBg[i].n for i in range(6)] if ch == 0
                    else b_w_base_prev)
        b_st_base = [s_stB[i].n for i in range(6)]
        for t in range(3):
            for g in range(G):
                idx = t * G + g
                for half in range(2):
                    hidx = idx * 2 + half
                    par = hidx % wb_n
                    waits = []
                    if hidx >= wb_n:
                        pj = hidx - wb_n
                        if pj % 2 == 0:
                            waits.append((s_peBh, b_peh_base + pj // 2 + 1))
                        else:
                            waits.append((s_peB, b_pe_base + pj // 2 + 1))
                    elif ch > 0:
                        waits.append((s_peB, b_pe_base))
                    if ch == 0:
                        if hidx >= 6:
                            waits.append((s_stB[par],
                                          b_st_base[par] + 16 * (hidx // 6)))
                        def fw(e, t=t, g=g, half=half, par=par):
                            return e.dma_start(
                                out=wbuf6[par][:, :],
                                in_=d_wT[t, g, :, half * 2048:(half + 1) * 2048])
                        op("gpsimd", fw, waits=waits, inc=s_wBg[par], amt=16)
                        def fs(e, t=t, g=g, half=half, par=par):
                            return e.dma_start(
                                out=d_wst[t, g, :, half * 2048:(half + 1) * 2048],
                                in_=wbuf6[par][:, :])
                        op("sync", fs,
                           waits=[(s_wBg[par], b_w_base[par] + 16 * (hidx // 6 + 1))],
                           inc=s_stB[par], amt=16)
                    elif hidx >= 3:
                        def fw(e, t=t, g=g, half=half, par=par):
                            return e.dma_start(
                                out=wbuf[par][:, :],
                                in_=d_wst[t, g, :, half * 2048:(half + 1) * 2048])
                        op("sync", fw, waits=waits, inc=s_wB[par], amt=16)
                pbank = pb[idx % 2]
                for s in range(G):
                    hidx = idx * 2 + s // 16
                    par = hidx % wb_n
                    waits = []
                    if s == 0:
                        if idx >= 2:
                            waits.append((s_actB, b_act_base + idx - 1))
                        elif ch > 0:
                            waits.append((s_actE, ch * 128))
                    if idx < 8:
                        waits.append((s_actA, a_cast_base + s + 1))
                    elif s == 0:
                        waits.append((s_actA, a_cast_base + G))
                    if s % 16 == 0:
                        waits.append((s_wBc[par],
                                      b_w_base[par] + 16 * (hidx // wb_n + 1)))
                    def fm(e, s=s, par=par, pbank=pbank, wb=wb):
                        return e.matmul(
                            pbank[:, :],
                            wb[par][:, (s % 16) * 128:(s % 16 + 1) * 128],
                            x0bf[:, s * T:(s + 1) * T],
                            start=(s == 0), stop=(s == G - 1))
                    inc = s_peB if s == G - 1 else (s_peBh if s == 15 else None)
                    op("tensor", fm, waits=waits, inc=inc)
                plane = (qT, kT, vT)[t]
                waits = [(s_peB, b_pe_base + idx + 1)]
                if idx == 0 and ch == 0:
                    waits.append((s_cst, 64))
                if t == 0 and g == 0 and ch > 0:
                    waits.append((s_x1, ch * 64))
                if t == 2 and g == 0 and ch > 0:
                    waits.append((s_out[0], ch * 32))
                    waits.append((s_out[1], ch * 32))
                def fe(e, t=t, g=g, plane=plane, pbank=pbank):
                    return e.activation(
                        out=plane[:, g:g + (T - 1) * G + 1:G],
                        in_=pbank[:, :], func=AF.Identity,
                        bias=b_sb2[:, t * G + g:t * G + g + 1], scale=1.0)
                op("scalar", fe, waits=waits, inc=s_actB)

        e_wf_base_early = [s_wF[0].n, s_wF[1].n]
        e_wd_base_early = [s_wD[0].n, s_wD[1].n]
        for hh in (0, 1):
            waits_u = [(s_udc, 16 * NH), (s_peB, s_peB.n)]
            waits_d = [(s_udc, 16 * NH), (s_peB, s_peB.n)]
            if ch > 0:
                waits_u.append((s_peU, ch * 128))
                waits_d.append((s_peDn, ch * 32))
            def fwfu(e, hh=hh):
                return e.dma_start(out=ubuf[hh][:, :],
                                   in_=d_ust[hh, :, 0:2048])
            op("sync", fwfu, waits=waits_u, inc=s_wF[hh], amt=16)
            def fwfd(e, hh=hh):
                return e.dma_start(out=dbuf[hh][:, :],
                                   in_=d_ust[hh, :, 2048:4096])
            op("sync", fwfd, waits=waits_d, inc=s_wD[hh], amt=16)

        c_pea_base = s_peCa.n
        c_peb_base = s_peCb.n
        c_acta_base = s_actCa.n
        c_dvev_base = s_dvev.n
        c_dver_base = s_dver.n
        c_dvee_base = s_dvee.n
        c_dvem_base = s_dveCm.n

        for i in (0, 1):
            def fz(e, i=i):
                return e.memset(probs_bd[i][:, :], 0.0)
            op("vector", fz, waits=[(s_peB, b_pe_base + 96)] if i == 0 else (),
               inc=s_dveCm)

        for k in range(NSB + 2):
            if k < NSB:
                waits = [(s_actB, b_act_base + 96)]
                if k == 0 and ch == 0:
                    waits.append((s_cst, 64))
                if k >= 2:
                    waits.append((s_actCa, c_acta_base + k - 1))
                    waits.append((s_dvev, c_dvev_base + k - 1))
                for j2 in range(4):
                    tok0 = k * 16 + j2 * 4
                    def ft(e, k=k, j2=j2, tok0=tok0):
                        return e.transpose(
                            pv[k % 2][:, j2 * 128:(j2 + 1) * 128],
                            vT[:, tok0 * G:(tok0 + 4) * G],
                            idbf_sb[:, :])
                    op("tensor", ft, waits=waits if j2 == 0 else ())
                for j2 in range(4):
                    for j in range(4):
                        tok = k * 16 + j2 * 4 + j
                        def fsc(e, k=k, j2=j2, j=j, tok=tok):
                            return e.matmul(
                                ps[k % 2][j * 32:(j + 1) * 32,
                                          j2 * 32:(j2 + 1) * 32],
                                kT[:, tok * G:tok * G + G],
                                qT[:, tok * G:tok * G + G],
                                start=True, stop=True,
                                tile_position=(0, j * 32))
                        op("tensor", fsc)
                op("tensor", lambda e: None, inc=s_peCa, drain=True)
            m = k - 1
            if 0 <= m < NSB:
                waits = [(s_actCa, c_acta_base + m + 1)]
                if m >= 2:
                    waits.append((s_dver, c_dver_base + m - 1))
                else:
                    waits.append((s_dveE, ch * 32))
                if ch == 0 and m == 0:
                    waits.append((s_selfV, cst_v))
                def fsum(e, m=m):
                    return e.matmul(pb[2 + m % 2][:, :], onesK[:, :],
                                    probs_bd[m % 2][:, :], start=True, stop=True)
                op("tensor", fsum, waits=waits)
                for j2 in range(4):
                    waits2 = []
                    if j2 == 0:
                        waits2.append((s_dvee, c_dvee_base + max(m - 1, 0)))
                        waits2.append((s_dvev, c_dvev_base + m + 1))
                    def fctx(e, m=m, j2=j2):
                        return e.matmul(
                            pb[4 + m % 2][:, j2 * 128:(j2 + 1) * 128],
                            vtok_sb[m % 2][:, j2 * 128:(j2 + 1) * 128],
                            probs_bd[m % 2][:, j2 * 128:(j2 + 1) * 128],
                            start=True, stop=True)
                    op("tensor", fctx, waits=waits2)
            op("tensor", lambda e: None, inc=s_peCb, drain=True)

        if ch == 0:
            for h in range(NH):
                def fudc(e, h=h):
                    return e.dma_start(out=d_ust[h, :, :], in_=d_updn[h, :, :])
                op("gpsimd", fudc, inc=s_udc, amt=16)

        op("scalar", load_table(TBL_LNEXP))
        for k in range(NSB + 1):
            if k < NSB:
                waits = [(s_peCa, c_pea_base + k + 1)]
                if k < 2:
                    waits.append((s_dveCm, c_dvem_base + 2))
                for j in range(4):
                    def fe2(e, k=k, j=j):
                        pr_r = probs_bd[k % 2].rearrange("p (j2 c) -> p j2 c",
                                                         j2=4)
                        ps_r = ps[k % 2][:, 0:128].rearrange(
                            "p (j2 h) -> p j2 h", j2=4)
                        return e.activation(
                            out=pr_r[j * 32:(j + 1) * 32, :, j * 32:(j + 1) * 32],
                            in_=ps_r[j * 32:(j + 1) * 32, :, :],
                            func=AF.Exp, scale=RSQ)
                    op("scalar", fe2, waits=waits if j == 0 else (),
                       inc=s_actCa if j == 3 else None)
            m = k - 1
            if 0 <= m < NSB:
                waits = [(s_peCb, c_peb_base + m + 2),
                         (s_dvee, c_dvee_base + max(m - 1, 0))]
                def fln(e, m=m):
                    return e.activation(out=rb_sb[m % 2][:, :],
                                        in_=pb[2 + m % 2][:, :], func=AF.Ln)
                op("scalar", fln, waits=waits, inc=s_selfD)
                svl = s_selfD.n
                def fex(e, m=m):
                    return e.activation(out=rb_sb[m % 2][:, :],
                                        in_=rb_sb[m % 2][:, :], func=AF.Exp,
                                        scale=-1.0)
                op("scalar", fex, waits=[(s_selfD, svl)], inc=s_dver)
        for k in range(NSB + 2):
            if k < NSB:
                def fv(e, k=k):
                    return e.tensor_copy(out=vtok_sb[k % 2][:, :],
                                         in_=pv[k % 2][:, :])
                op("vector", fv, waits=[(s_peCa, c_pea_base + k + 1)],
                   inc=s_dvev)
            m2 = k - 2
            if 0 <= m2 < NSB:
                for j2 in range(4):
                    waits = [(s_peCb, c_peb_base + k)]
                    if j2 == 0:
                        waits.append((s_dver, c_dver_base + m2 + 1))
                    if ch == 0 and m2 == 0 and j2 == 0:
                        for i in (3, 4, 5):
                            waits.append((s_stB[i], 16 * 32))
                    def fcm(e, m2=m2, j2=j2):
                        tok0 = m2 * 16 + j2 * 4
                        dst = sumT.rearrange("p (h tok) -> p tok h", h=G)[
                            :, tok0:tok0 + 4, :]
                        src = pb[4 + m2 % 2][:, j2 * 128:(j2 + 1) * 128].rearrange(
                            "p (j h) -> p j h", j=4)
                        rbs = rb_sb[m2 % 2][:, j2 * 128:(j2 + 1) * 128].rearrange(
                            "p (j h) -> p j h", j=4)
                        return e.tensor_tensor(out=dst, in0=rbs, in1=src,
                                               op=OP.mult)
                    last = (j2 == 3)
                    op("vector", fcm, waits=waits,
                       inc=s_dvee if last else None)
        d_pe_base = s_peD.n
        d_as_base = s_actDs.n
        d_da_base = s_dveDa.n
        d_dc_base = s_dveDc.n
        d_dn_base = s_dveDn.n
        d_ax_base = s_actDx.n
        d_xd_base = [x.n for x in s_xd]
        for g in range(G):
            par4 = g % 8
            par2 = g % 2
            waits = []
            if g >= 8:
                waits.append((s_dveDa, d_da_base + g - 7))
            if g < 8:
                waits.append((s_actA, a_cast_base + G))
                waits.append((s_peCb, c_peb_base + NSB + 2))
            def fxl(e, g=g, par4=par4, ch_tok=ch_tok):
                return e.dma_start(out=x0Tbuf[par4][:, :],
                                   in_=d_x0T[g * 128:(g + 1) * 128,
                                             ch_tok:ch_tok + T])
            op("sync", fxl, waits=waits, inc=s_xd[par4], amt=16)
            waits = [(s_xd[par4], d_xd_base[par4] + 16 * (g // 8 + 1))]
            if g == 0:
                waits.append((s_dvee, c_dvee_base + NSB))
            def fadd(e, g=g, par4=par4):
                return e.tensor_tensor(out=sumT[:, g * T:(g + 1) * T],
                                       in0=sumT[:, g * T:(g + 1) * T],
                                       in1=x0Tbuf[par4][:, :], op=OP.add)
            op("vector", fadd, waits=waits, inc=s_dveDa)
            sda = s_dveDa.n
            waits = [(s_dveDa, sda)]
            if g >= 2:
                waits.append((s_peD, d_pe_base + g - 1))
            def fcast(e, g=g, par2=par2):
                return e.tensor_copy(out=xbf_scr[par2][:, :],
                                     in_=sumT[:, g * T:(g + 1) * T])
            op("vector", fcast, waits=waits, inc=s_dveDc)
            waits = [(s_dveDa, sda)]
            if g >= 2:
                waits.append((s_peD, d_pe_base + g - 1))
            def fsq(e, g=g, par2=par2):
                return e.activation(out=sq_scr[par2][:, :],
                                    in_=sumT[:, g * T:(g + 1) * T],
                                    func=AF.Square)
            op("scalar", fsq, waits=waits, inc=s_actDs)
            waits = [(s_dveDc, d_dc_base + g + 1), (s_actDs, d_as_base + g + 1)]
            if g == 0:
                waits.append((s_dver, c_dver_base + NSB))
                if ch == 0:
                    waits.append((s_selfV, cst_v))
            def fm1(e, g=g, par2=par2):
                return e.matmul(pb[2][0:1, :], ones128[:, :], xbf_scr[par2][:, :],
                                start=(g == 0), stop=(g == G - 1))
            op("tensor", fm1, waits=waits)
            def fm2(e, g=g, par2=par2):
                return e.matmul(pb[3][0:1, :], ones128[:, :], sq_scr[par2][:, :],
                                start=(g == 0), stop=(g == G - 1))
            op("tensor", fm2, inc=s_peD)
        def fr1(e):
            return e.tensor_scalar(out=row_mean[:, :], in0=pb[2][0:1, :],
                                   scalar1=1.0 / 4096.0, scalar2=None, op0=OP.mult)
        op("vector", fr1, waits=[(s_peD, d_pe_base + G)], inc=s_selfV)
        def fr2(e):
            return e.tensor_scalar(out=row_var[:, :], in0=pb[3][0:1, :],
                                   scalar1=1.0 / 4096.0, scalar2=None, op0=OP.mult)
        op("vector", fr2, inc=s_selfV)
        sd2 = s_selfV.n
        def fr3(e):
            return e.tensor_tensor(out=row_A[:, :], in0=row_mean[:, :],
                                   in1=row_mean[:, :], op=OP.mult)
        op("vector", fr3, waits=[(s_selfV, sd2)], inc=s_selfV)
        sd3 = s_selfV.n
        def fr4(e):
            return e.tensor_tensor(out=row_var[:, :], in0=row_var[:, :],
                                   in1=row_A[:, :], op=OP.subtract)
        op("vector", fr4, waits=[(s_selfV, sd3)], inc=s_selfV)
        sd4 = s_selfV.n
        def fr5(e):
            return e.activation(out=row_B[:, :], in_=row_var[:, :], func=AF.Ln,
                                bias=eps_sb[0:1, 0:1], scale=1.0)
        op("scalar", fr5, waits=[(s_selfV, sd4)], inc=s_selfD)
        sd5 = s_selfD.n
        def fr6(e):
            return e.activation(out=row_A[:, :], in_=row_B[:, :], func=AF.Exp,
                                scale=-0.5)
        op("scalar", fr6, waits=[(s_selfD, sd5)], inc=s_selfD)
        sd6 = s_selfD.n
        def fr7(e):
            return e.tensor_tensor(out=row_B[:, :], in0=row_mean[:, :],
                                   in1=row_A[:, :], op=OP.mult)
        op("vector", fr7, waits=[(s_selfD, sd6)], inc=s_selfV)
        sd7 = s_selfV.n
        def fr8(e):
            return e.tensor_scalar(out=row_B[:, :], in0=row_B[:, :],
                                   scalar1=-1.0, scalar2=None, op0=OP.mult)
        op("vector", fr8, waits=[(s_selfV, sd7)], inc=s_selfV)
        sd8 = s_selfV.n
        def fhiA(e):
            return e.tensor_copy(out=row_hi[:, :], in_=row_A[:, :])
        op("vector", fhiA, waits=[(s_selfV, sd8)], inc=s_selfV)
        sd9 = s_selfV.n
        def floA(e):
            return e.tensor_tensor(out=row_lo[:, :], in0=row_A[:, :],
                                   in1=row_hi[:, :], op=OP.subtract)
        op("vector", floA, waits=[(s_selfV, sd9)], inc=s_selfV)
        sd10 = s_selfV.n
        def fbA1(e):
            return e.matmul(pb[0][:, :], ones1w[:, :], row_hi[:, :],
                            start=True, stop=False)
        op("tensor", fbA1, waits=[(s_selfV, sd10),
                                  (s_dvee, c_dvee_base + NSB)])
        def fbA2(e):
            return e.matmul(pb[0][:, :], ones1w[:, :], row_lo[:, :],
                            start=False, stop=True)
        op("tensor", fbA2, inc=s_peDb1, drain=True)
        db1 = s_peDb1.n
        def fhiB(e):
            return e.tensor_copy(out=row_hi[:, :], in_=row_B[:, :])
        op("vector", fhiB, waits=[(s_peDb1, db1)], inc=s_selfV)
        sd11 = s_selfV.n
        def floB(e):
            return e.tensor_tensor(out=row_lo[:, :], in0=row_B[:, :],
                                   in1=row_hi[:, :], op=OP.subtract)
        op("vector", floB, waits=[(s_selfV, sd11)], inc=s_selfV)
        sd12 = s_selfV.n
        def fbB1(e):
            return e.matmul(pb[1][:, :], ones1w[:, :], row_hi[:, :],
                            start=True, stop=False)
        op("tensor", fbB1, waits=[(s_selfV, sd12)])
        def fbB2(e):
            return e.matmul(pb[1][:, :], ones1w[:, :], row_lo[:, :],
                            start=False, stop=True)
        op("tensor", fbB2, inc=s_peDb2, drain=True)
        db2 = s_peDb2.n
        for g in range(G):
            waits = [(s_peDb2, db2)] if g == 0 else ()
            def fn1(e, g=g):
                return e.tensor_tensor(out=sumT[:, g * T:(g + 1) * T],
                                       in0=sumT[:, g * T:(g + 1) * T],
                                       in1=pb[0][:, :], op=OP.mult)
            op("vector", fn1, waits=waits, inc=s_selfV)
            sdg = s_selfV.n
            def fn2(e, g=g):
                return e.tensor_tensor(out=sumT[:, g * T:(g + 1) * T],
                                       in0=sumT[:, g * T:(g + 1) * T],
                                       in1=pb[1][:, :], op=OP.add)
            op("vector", fn2, waits=[(s_selfV, sdg)], inc=s_dveDn)
            def fxc(e, g=g):
                return e.activation(out=x1bf[:, g * T:(g + 1) * T],
                                    in_=sumT[:, g * T:(g + 1) * T],
                                    func=AF.Copy)
            op("scalar", fxc, waits=[(s_dveDn, d_dn_base + g + 1)], inc=s_actDx)
        d_x1v = d_x1bf.rearrange("(g p) tok -> p g tok", p=128)
        x1bf_v = x1bf.rearrange("p (g tok) -> p g tok", g=G)
        for piece in range(4):
            def fx1(e, piece=piece):
                return e.dma_start(
                    out=d_x1v[:, piece * 8:(piece + 1) * 8, :],
                    in_=x1bf_v[:, piece * 8:(piece + 1) * 8, :])
            op("sync", fx1, waits=[(s_actDx, d_ax_base + 8 * (piece + 1))],
               inc=s_x1, amt=16)

        s_wFc, s_wDc = s_wF, s_wD
        e_wf_base = e_wf_base_early
        e_wd_base = e_wd_base_early

        e_g_base = [x.n for x in s_g]
        e_peU_base = s_peU.n
        e_actE_base = s_actE.n
        e_peDn_base = s_peDn.n
        e_dveE_base = s_dveE.n
        e_sc_base = [x.n for x in s_sc]
        op("scalar", load_table(TBL_GELU))

        def emit_gathers(hh):
            for c2 in range(2):
                waits = [(s_x1, 16 * 4 * (ch + 1))]
                if hh == 0 and c2 == 0 and ch == 0:
                    waits.append((s_cst, 64))
                if hh >= 4:
                    waits.append((s_peU, e_peU_base + 8 * (hh - 3)))
                def fg(e, hh=hh, c2=c2):
                    return e.indirect_dma_start(
                        out=xg[hh % 4][:, c2 * T:(c2 + 1) * T], out_offset=None,
                        in_=d_x1bf[:, :],
                        in_offset=bass.IndirectOffsetOnAxis(
                            ap=perm_sb[:, 2 * hh + c2:2 * hh + c2 + 1], axis=0))
                op("gpsimd", fg, waits=waits, inc=s_g[hh % 4], amt=16)

        for hh in range(4):
            emit_gathers(hh)
        for h in range(NH):
            par = h % 2
            waits = [(s_udc, 16 * NH)]
            if h >= 2:
                waits.append((s_peU, e_peU_base + 8 * (h - 1)))
            else:
                waits.append((s_peB, b_pe_base + 96))
            if h >= 2:
                def fwfu(e, h=h, par=par):
                    return e.dma_start(out=ubuf[par][:, :], in_=d_ust[h, :, 0:2048])
                op("sync", fwfu, waits=waits, inc=s_wF[par], amt=16)
            waits = [(s_udc, 16 * NH)]
            if h >= 2:
                waits.append((s_peDn, e_peDn_base + 2 * (h - 1)))
            else:
                waits.append((s_peB, b_pe_base + 96))
            if h >= 2:
                def fwfd(e, h=h, par=par):
                    return e.dma_start(out=dbuf[par][:, :],
                                       in_=d_ust[h, :, 2048:4096])
                op("sync", fwfd, waits=waits, inc=s_wD[par], amt=16)
            if h + 4 < NH:
                emit_gathers(h + 4)
            for mt in range(8):
                ubank = (pb[0], pb[1], pb[4], pb[5])[mt % 4]
                waits = [(s_wFc[par], e_wf_base[par] + 16 * (h // 2 + 1)),
                         (s_g[h % 4], e_g_base[h % 4] + 32 * (h // 4 + 1))]
                if h * 8 + mt >= 4:
                    waits.append((s_actE, e_actE_base + h * 8 + mt - 3))
                elif mt < 2:
                    waits.append((s_dveDn, d_dn_base + G))
                else:
                    waits.append((s_dvee, c_dvee_base + NSB))
                for ct in range(2):
                    def fu(e, h=h, mt=mt, ct=ct, par=par, ubank=ubank):
                        return e.matmul(
                            ubank[:, :],
                            ubuf[par][:, ct * 1024 + mt * 128:
                                      ct * 1024 + (mt + 1) * 128],
                            xg[h % 4][:, ct * T:(ct + 1) * T],
                            start=(ct == 0), stop=(ct == 1))
                    op("tensor", fu, waits=waits if ct == 0 else (),
                       inc=s_peU if ct == 1 else None, drain=(ct == 1))
                def fgel(e, mt=mt, par=par, ubank=ubank):
                    return e.activation(out=gelu_sb[par][:, mt * T:(mt + 1) * T],
                                        in_=ubank[:, :], func=AF.Gelu)
                op("scalar", fgel, waits=[(s_peU, e_peU_base + h * 8 + mt + 1)],
                   inc=s_actE)
            for dt in range(2):
                dbank = pb[2 + dt]
                waits = [(s_actE, e_actE_base + 8 * (h + 1)),
                         (s_wDc[par], e_wd_base[par] + 16 * (h // 2 + 1))]
                if h >= 1:
                    waits.append((s_dveE, e_dveE_base + 2 * (h - 1) + dt + 1))
                else:
                    waits.append((s_selfV, sd2))
                for mt in range(8):
                    def fd(e, dt=dt, mt=mt, par=par, dbank=dbank):
                        return e.matmul(
                            dbank[:, :],
                            dbuf[par][:, mt * 256 + dt * 128:
                                      mt * 256 + (dt + 1) * 128],
                            gelu_sb[par][:, mt * T:(mt + 1) * T],
                            start=(mt == 0), stop=(mt == 7))
                    op("tensor", fd, waits=waits if mt == 0 else (),
                       inc=s_peDn if mt == 7 else None, drain=(mt == 7))
                waits = [(s_peDn, e_peDn_base + 2 * h + dt + 1)]
                fb4 = (2 * h + dt) % 4
                if h >= 2:
                    waits.append((s_sc[fb4], e_sc_base[fb4] + 16 * (h // 2)))
                def fde(e, dt=dt, fb4=fb4, dbank=dbank):
                    return e.tensor_copy(out=ffn_st[fb4][:, :], in_=dbank[:, :])
                op("vector", fde, waits=waits, inc=s_dveE)
                def fsc2(e, h=h, dt=dt, fb4=fb4):
                    return e.indirect_dma_start(
                        out=d_ffnat[:, :],
                        out_offset=bass.IndirectOffsetOnAxis(
                            ap=perm_sb[:, 2 * h + dt:2 * h + dt + 1], axis=0),
                        in_=ffn_st[fb4][:, :], in_offset=None)
                op("gpsimd", fsc2,
                   waits=[(s_dveE, e_dveE_base + 2 * h + dt + 1)],
                   inc=s_sc[fb4], amt=16)

        b_w_base_prev = [s_wB[0].n, s_wB[1].n, s_wB[2].n]
        if ch + 1 < NCH:
            for hidx in range(3):
                par = hidx % 3
                t_n, g_n, half_n = hidx // 64, (hidx // 2) % G, hidx % 2
                waits = [(s_peB, 96 * (ch + 1))]
                if ch == 0:
                    for i in range(6):
                        waits.append((s_stB[i], 16 * 32))
                def fwe(e, t_n=t_n, g_n=g_n, half_n=half_n, par=par):
                    return e.dma_start(
                        out=wbuf[par][:, :],
                        in_=d_wst[t_n, g_n, :,
                                  half_n * 2048:(half_n + 1) * 2048])
                op("sync", fwe, waits=waits, inc=s_wB[par], amt=16)

        s6_fa_base = [x.n for x in s_fa]
        s6_p6_base = s_pe6.n
        s6_e_base = s_dve6e.n
        s6_n_base = s_act6n.n
        s6_out_base = [s_out[0].n, s_out[1].n]
        if ch + 1 < NCH:
            a_ld_base_next = emit_a_loads(ch + 1, "sync")
            emit_a_casts(ch + 1, a_ld_base_next, 0, G)
        op("scalar", load_table(TBL_SQRT))
        for k in range(G // 2):
            par = k % 3
            waits = [(s_sc[i], e_sc_base[i] + 16 * 8) for i in range(4)]
            if k >= 3:
                waits.append((s_fa[par], s6_fa_base[par] + 16 * (k // 3)))
            def ffb(e, k=k):
                return e.dma_start(
                    out=sumT.rearrange("p (g t) -> p g t", g=G)[
                        :, 2 * k:2 * k + 2, :],
                    in_=d_ffnat[256 * k:256 * (k + 1), :].rearrange(
                        "(b p) t -> p b t", p=128),
                    accum_op=OP.add)
            op("gpsimd", ffb, waits=waits, inc=s_fa[par], amt=16)
        for tt in range(4):
            for q in range(8):
                qidx = tt * 8 + q
                par = qidx % 2
                k0, k1 = 2 * q, 2 * q + 1
                waits = [(s_fa[k0 % 3], s6_fa_base[k0 % 3] + 16 * (k0 // 3 + 1)),
                         (s_fa[k1 % 3], s6_fa_base[k1 % 3] + 16 * (k1 // 3 + 1))]
                if qidx == 0:
                    waits.append((s_dveDn, d_dn_base + G))
                    waits.append((s_actE, e_actE_base + 8 * NH))
                    if ch == 0:
                        waits.append((s_cst, 64))
                if qidx >= 2:
                    waits.append((s_dve6e, s6_e_base + qidx - 1))
                for gg in range(4):
                    g = q * 4 + gg
                    def ftr(e, tt=tt, g=g, gg=gg, par=par):
                        return e.transpose(
                            pb[6 + par][:, gg * 128:(gg + 1) * 128],
                            sumT[:, g * T + tt * 128:g * T + (tt + 1) * 128],
                            id32_sb[:, :])
                    op("tensor", ftr, waits=waits if gg == 0 else ())
                op("tensor", lambda e: None, inc=s_pe6, drain=True)
                waits = [(s_pe6, s6_p6_base + qidx + 1)]
                if q == 0:
                    if tt >= 2:
                        waits.append((s_out[tt % 2],
                                      s6_out_base[tt % 2] + 16 * (tt // 2)))
                    else:
                        for i in range(4):
                            waits.append((s_sc[i], e_sc_base[i] + 16 * 8))
                def fev(e, tt=tt, q=q, par=par):
                    return e.tensor_copy(
                        out=out_st[tt % 2][:, q * 512:(q + 1) * 512],
                        in_=pb[6 + par][:, :])
                op("vector", fev, waits=waits, inc=s_dve6e)
            for jj in range(8):
                def fbn1(e, tt=tt, jj=jj):
                    return e.bn_stats(out=s6_bn[:, jj * 6:(jj + 1) * 6],
                                      in_=out_st[tt % 2][:, jj * 512:(jj + 1) * 512])
                op("vector", fbn1,
                   waits=[(s_dve6e, s6_e_base + (tt + 1) * 8)] if jj == 0 else (),
                   inc=s_self6)
            sb8 = s_self6.n
            def fagg(e, tt=tt):
                return e.bn_aggr(out=s6_mv[:, :], in_=s6_bn[:, :])
            op("vector", fagg, waits=[(s_self6, sb8)], inc=s_self6)
            sb9 = s_self6.n
            def fstd(e, tt=tt):
                return e.activation(out=s6_std[:, :], in_=s6_mv[:, 1:2],
                                    func=AF.Sqrt, bias=eps_sb[:, 0:1], scale=1.0)
            op("scalar", fstd, waits=[(s_self6, sb9)], inc=s_self6)
            sb10 = s_self6.n
            def frst(e, tt=tt):
                return e.reciprocal(out=s6_rstd[:, :], in_=s6_std[:, :])
            op("vector", frst, waits=[(s_self6, sb10)], inc=s_self6)
            sb11 = s_self6.n
            def fnm(e, tt=tt):
                return e.tensor_tensor(out=s6_nmr[:, :], in0=s6_mv[:, 0:1],
                                       in1=s6_rstd[:, :], op=OP.mult)
            op("vector", fnm, waits=[(s_self6, sb11)], inc=s_self6)
            sb12 = s_self6.n
            def fng(e, tt=tt):
                return e.tensor_scalar(out=s6_nmr[:, :], in0=s6_nmr[:, :],
                                       scalar1=-1.0, scalar2=None, op0=OP.mult)
            op("vector", fng, waits=[(s_self6, sb12)], inc=s_self6)
            sb13 = s_self6.n
            def fnorm(e, tt=tt):
                return e.activation(out=out_st[tt % 2][:, :],
                                    in_=out_st[tt % 2][:, :], func=AF.Identity,
                                    bias=s6_nmr[:, 0:1], scale=s6_rstd[:, 0:1])
            op("scalar", fnorm, waits=[(s_self6, sb13)], inc=s_act6n)
            def fout(e, tt=tt, ch=ch):
                return e.dma_start(
                    out=d_out[(ch * 4 + tt) * 128:(ch * 4 + tt + 1) * 128, :],
                    in_=out_st[tt % 2][:, :])
            op("sync", fout, waits=[(s_act6n, s6_n_base + tt + 1)],
               inc=s_out[tt % 2], amt=16)

    op("sync", lambda e: e.nop(),
       waits=[(s_out[0], s_out[0].n), (s_out[1], s_out[1].n)])

    blk = nc.Block().__enter__()

    import bass_rust

    def emit_engine(eng, name):
        evented = set()
        for o in OPL[name]:
            for (sem, val) in o.waits:
                eng.wait_ge(sem.h, val)
            instr = o.fn(eng)
            if o.incs:
                if instr is None:
                    raw = nc.cur_bb.bb.instructions[-1]
                    if raw.name in evented:
                        for (sem, amt, tgt) in o.incs:
                            eng.wait_ge(sem.h, tgt - amt)
                            eng.sem_inc(sem.h, amt)
                        continue
                    evented.add(raw.name)
                    instr = bass_rust.BassInstruction(raw)
                else:
                    try:
                        evented.add(instr.ins.name)
                    except Exception:
                        pass
                for (sem, amt, tgt) in o.incs:
                    instr.then_inc(sem.h, amt)

    @blk.sync
    def _(e):
        emit_engine(e, "sync")

    @blk.tensor
    def _(e):
        emit_engine(e, "tensor")

    @blk.scalar
    def _(e):
        emit_engine(e, "scalar")

    @blk.vector
    def _(e):
        emit_engine(e, "vector")

    @blk.gpsimd
    def _(e):
        emit_engine(e, "gpsimd")

    blk.__exit__(None, None, None)
    return nc


# version 25
# speedup vs baseline: 1.0573x; 1.0573x over previous
import numpy as np
import concourse.bass as bass
import concourse.mybir as mybir
from concourse.alu_op_type import AluOpType
from concourse.bass_utils import run_bass_kernel_spmd
from concourse.hw_specs import get_activation_tables

F32 = mybir.dt.float32
BF16 = mybir.dt.bfloat16
U32 = mybir.dt.uint32
AF = mybir.ActivationFunctionType
OP = AluOpType

NCORES = 8
NTOK = 2048
T = 512
NCH = NTOK // T
G = 32
NSB = T // 16
NH = 16
EPS = 1e-5
RSQ = float(1.0 / np.sqrt(128.0))


class Sem:
    def __init__(self, nc, name):
        self.h = nc.semaphore(name).__enter__()
        self.n = 0

    def bump(self, amt):
        self.n += amt
        return self.n


class Op:
    __slots__ = ("waits", "fn", "incs", "drain")

    def __init__(self, waits, fn, incs, drain):
        self.waits, self.fn, self.incs, self.drain = waits, fn, incs, drain


def kernel(L0_attn_hidden_states, L0_attn_wq, L0_attn_bq, L0_attn_wk, L0_attn_bk,
           L0_attn_wv, L0_attn_bv, L0_attn_wl, L0_attn_bl,
           L0_ffn_up, L0_ffn_down, L0_ffn_perm):
    import ml_dtypes
    x = np.ascontiguousarray(np.asarray(L0_attn_hidden_states, dtype=np.float32)
                             .reshape(NCORES * NTOK, 4096))

    def blockT(w):
        w4 = np.asarray(w, np.float32).reshape(G, 128, G, 128)
        return np.ascontiguousarray(w4.transpose(0, 3, 2, 1).reshape(G, 128, 4096))

    wT = np.ascontiguousarray(np.stack(
        [blockT(L0_attn_wq), blockT(L0_attn_wk), blockT(L0_attn_wv)]))
    bias = np.ascontiguousarray(
        np.stack([np.ascontiguousarray(np.asarray(b, np.float32).reshape(G, 128).T)
                  for b in (L0_attn_bq, L0_attn_bk, L0_attn_bv)]))
    up5 = np.asarray(L0_ffn_up, np.float32).reshape(NH, 2, 128, 1024) \
        .transpose(0, 2, 1, 3).reshape(NH, 128, 2048)
    dn5 = np.asarray(L0_ffn_down, np.float32).reshape(NH, 8, 128, 256) \
        .transpose(0, 2, 1, 3).reshape(NH, 128, 2048)
    updn = np.ascontiguousarray(np.concatenate([up5, dn5], axis=2))
    permT = np.ascontiguousarray(
        np.asarray(L0_ffn_perm).astype(np.int64).astype(np.uint32)
        .reshape(G, 128).T)
    id32 = np.eye(128, dtype=np.float32)
    idbf = np.eye(128, dtype=ml_dtypes.bfloat16)

    nc = build_program()

    in_maps = []
    for i in range(NCORES):
        xT = np.ascontiguousarray(x[i * NTOK:(i + 1) * NTOK].T)
        in_maps.append({"x0T": xT, "wT": wT, "bias": bias, "updn": updn,
                        "permT": permT, "id32": id32, "idbf": idbf})
    res = run_bass_kernel_spmd(nc, in_maps, core_ids=list(range(NCORES)))
    out = np.concatenate([r["out"] for r in res.results], axis=0)
    return out.reshape(2, 8192, 4096)


def build_program():
    nc = bass.Bass("TRN2", target_bir_lowering=False)
    OPL = {k: [] for k in ("sync", "tensor", "scalar", "vector", "gpsimd")}

    d_x0T = nc.dram_tensor("x0T", [4096, NTOK], F32, kind="ExternalInput")
    d_wT = nc.dram_tensor("wT", [3, G, 128, 4096], F32, kind="ExternalInput")
    d_bias = nc.dram_tensor("bias", [3, 128, G], F32, kind="ExternalInput")
    d_updn = nc.dram_tensor("updn", [NH, 128, 4096], F32, kind="ExternalInput")
    d_perm = nc.dram_tensor("permT", [128, G], U32, kind="ExternalInput")
    d_id32 = nc.dram_tensor("id32", [128, 128], F32, kind="ExternalInput")
    d_idbf = nc.dram_tensor("idbf", [128, 128], BF16, kind="ExternalInput")
    d_out = nc.dram_tensor("out", [NTOK, 4096], F32, kind="ExternalOutput")
    d_wst = nc.dram_tensor("wstash", [3, G, 128, 4096], BF16, kind="Internal")
    d_ust = nc.dram_tensor("updnstash", [NH, 128, 4096], BF16, kind="Internal")
    d_x1bf = nc.dram_tensor("x1bf", [4096, T], BF16, kind="Internal")
    d_ffnat = nc.dram_tensor("ffnat", [4096, T], F32, kind="Internal")

    arena_items = []
    arena_size = [0]

    def sb_at(off, name, shape, dtype):
        import functools, operator
        per_part = functools.reduce(operator.mul, shape[1:], 1) * mybir.dt.size(dtype)
        assert off % 32 == 0, name
        arena_size[0] = max(arena_size[0], off + per_part)
        h = nc.virtual_sbuf_tensor(name, shape, dtype)
        arena_items.append((h, off))
        return h

    sumT = sb_at(0, "sumT", [128, G * T], F32)
    qT = sb_at(65536, "qT", [128, G * T], BF16)
    kT = sb_at(98304, "kT", [128, G * T], BF16)
    x1bf = sb_at(65536, "x1bfp", [128, G * T], BF16)
    xbf_scr = [sb_at(98304 + i * 1024, f"xbfs{i}", [128, T], BF16) for i in (0, 1)]
    sq_scr = [sb_at(100352 + i * 1024, f"sqs{i}", [128, T], BF16) for i in (0, 1)]
    x0Tbuf = [sb_at(102400 + i * 2048, f"x0Tb{i}", [128, T], F32)
              for i in range(8)]
    vT = sb_at(131072, "vT", [128, G * T], BF16)
    gelu_sb = [sb_at(131072 + i * 8192, f"gelu{i}", [128, 8 * T], BF16)
               for i in (0, 1)]
    xg = [sb_at(147456 + i * 2048, f"xg{i}", [128, 2 * T], BF16)
          for i in (0, 1)] + [sb_at(155648 + i * 2048, f"xg{2+i}", [128, 2 * T],
                                    BF16) for i in (0, 1)]
    ffn_st = [sb_at(151552 + i * 2048, f"ffnst{i}", [128, T], F32)
              for i in (0, 1)] + [sb_at(159744 + i * 2048, f"ffnst{2+i}",
                                        [128, T], F32) for i in (0, 1)]
    out_st = [sb_at(131072 + i * 16384, f"outst{i}", [128, 4096], F32)
              for i in (0, 1)]
    x0bf = sb_at(163840, "x0bf", [128, G * T], BF16)
    ubuf = [sb_at(163840 + i * 4096, f"ubuf{i}", [128, 2048], BF16)
            for i in (0, 1)]
    dbuf = [sb_at(172032 + i * 4096, f"dbuf{i}", [128, 2048], BF16)
            for i in (0, 1)]
    fbuf = [sb_at(172032 + i * 4096, f"fbuf{i}", [128, 2 * T], F32)
            for i in range(3)]
    probs_bd = [sb_at(184320 + i * 1024, f"probsbd{i}", [128, 4 * 128], BF16)
                for i in (0, 1)]
    vtok_sb = [sb_at(186368 + i * 1024, f"vtok{i}", [128, 4 * 128], BF16)
               for i in (0, 1)]
    rb_sb = [sb_at(188416, "rb0", [128, T], F32),
             sb_at(190464, "rb1", [128, T], F32)]
    onesK = sb_at(210816, "onesK", [128, 128], BF16)
    row_mean = sb_at(184320, "rowmean", [1, T], F32)
    row_var = sb_at(186368, "rowvar", [1, T], F32)
    row_A = sb_at(188416, "rowA", [1, T], F32)
    row_B = sb_at(190464, "rowB", [1, T], F32)
    row_hi = sb_at(192512, "rowhi", [1, T], BF16)
    row_lo = sb_at(193536, "rowlo", [1, T], BF16)
    wbuf = [sb_at(196608 + i * 4096, f"wbuf{i}", [128, 2048], BF16)
            for i in (0, 1, 2)]
    b_sb2 = sb_at(208896, "b_sb2", [128, 3 * G], F32)
    perm_sb = sb_at(209280, "perm_sb", [128, G], U32)
    id32_sb = sb_at(209408, "id32_sb", [128, 128], F32)
    idbf_sb = sb_at(209920, "idbf_sb", [128, 128], BF16)
    ones128 = sb_at(210176, "ones128", [128, 1], BF16)
    ones1w = sb_at(210208, "ones1w", [1, 128], BF16)
    s6_std = sb_at(210464, "s6_std", [128, 1], F32)
    s6_rstd = sb_at(210496, "s6_rstd", [128, 1], F32)
    s6_nmr = sb_at(210528, "s6_nmr", [128, 1], F32)
    s6_bn = sb_at(210560, "s6_bn", [128, 48], F32)
    s6_mv = sb_at(210752, "s6_mv", [128, 2], F32)
    eps_sb = sb_at(210784, "eps_sb", [128, 1], F32)
    assert arena_size[0] <= 212800, arena_size[0]

    backing = nc.sbuf_tensor("arena", [128, arena_size[0]],
                             mybir.dt.uint8).__enter__()
    base_addr = nc.lookup_mloc(backing).addr
    for h, off in arena_items:
        nc._tensor(h.name, h.shape, h.dtype, type="SB")
        m = nc.lookup_mloc(h)
        m.allocated = True
        m.addr = base_addr + off

    pb = [nc.psum_tensor(f"pb{i}", [128, 512], F32).__enter__() for i in range(8)]
    pv = []
    for i in (0, 1):
        mb = nc.lookup_mloc(pb[6 + i])
        h1 = nc.virtual_psum_tensor(f"pv{i}", [128, 512], BF16)
        nc._tensor(h1.name, h1.shape, h1.dtype, type="PSUM")
        m = nc.lookup_mloc(h1)
        m.allocated = True; m.addr = mb.addr + 0; m.bank = mb.bank
        pv.append(h1)
    ps = [pb[0], pb[1]]

    S = lambda n: Sem(nc, n)
    s_cst = S("cst")
    s_ld = [S(f"ld{i}") for i in range(8)]
    s_ldg = [S(f"ldg{i}") for i in range(8)]
    s_wB = [S("wB0"), S("wB1"), S("wB2")]
    s_wBg = [S("wBg0"), S("wBg1"), S("wBg2")]
    s_stB = [S("stB0"), S("stB1"), S("stB2")]
    s_wF = [S("wF0"), S("wF1")]
    s_wFg = [S("wFg0"), S("wFg1")]
    s_wD = [S("wD0"), S("wD1")]
    s_udc = S("udc")
    s_stF = [S("stF0"), S("stF1")]
    s_g = [S("g0"), S("g1"), S("g2"), S("g3")]
    s_xd = [S(f"xd{i}") for i in range(8)]
    s_fb = [S(f"fb{i}") for i in range(3)]
    s_fa = [S(f"fa{i}") for i in range(3)]
    s_sc = [S("sc0"), S("sc1"), S("sc2"), S("sc3")]
    s_out = [S("out0"), S("out1")]
    s_x1 = S("x1")
    s_peB, s_peBh, s_actB, s_actA = S("peB"), S("peBh"), S("actB"), S("actA")
    s_peCa, s_peCb = S("peCa"), S("peCb")
    s_actCa, s_dvev, s_dveCm = S("actCa"), S("dvev"), S("dveCm")
    s_dver, s_dvee = S("dver"), S("dvee")
    s_peD, s_peDb1, s_peDb2 = S("peD"), S("peDb1"), S("peDb2")
    s_actDs, s_dveDa, s_dveDc, s_dveDn, s_actDx = \
        S("actDs"), S("dveDa"), S("dveDc"), S("dveDn"), S("actDx")
    s_gpDn = S("gpDn")
    s_gpSelf = S("gpSelf")
    s_peU, s_actE, s_peDn, s_dveE = S("peU"), S("actE"), S("peDn"), S("dveE")
    s_dve6, s_pe6, s_dve6e, s_act6n = S("dve6"), S("pe6"), S("dve6e"), S("act6n")
    s_selfD, s_selfV, s_self6 = S("selfD"), S("selfV"), S("self6")

    tables = list(get_activation_tables(nc.m.arch).keys())
    TBL_LNEXP = tables.index("natural_log_exp_and_others")
    TBL_GELU = tables.index("gelu_and_others")
    TBL_SQRT = tables.index("sqrt_and_others")

    def load_table(idx):
        def f(e):
            return e.add_instruction(mybir.InstLoadActFuncSet(
                name=nc.get_next_instruction_name(), act_func_set_id=idx,
                ins=[], outs=[]))
        return f

    def op(engine, fn, waits=(), inc=None, amt=1, drain=False):
        incs = []
        if inc is not None:
            inc.bump(amt)
            incs.append((inc, amt, inc.n))
        OPL[engine].append(Op([(s, v) for (s, v) in waits if v > 0],
                              fn, incs, drain))

    op("sync", lambda e: e.dma_start(
        out=b_sb2.rearrange("p (t g) -> p t g", t=3),
        in_=d_bias.rearrange("t p g -> p t g")),
        inc=s_cst, amt=16)
    op("sync", lambda e: e.dma_start(out=perm_sb[:, :], in_=d_perm[:, :]),
       inc=s_cst, amt=16)
    op("sync", lambda e: e.dma_start(out=id32_sb[:, :], in_=d_id32[:, :]),
       inc=s_cst, amt=16)
    op("sync", lambda e: e.dma_start(out=idbf_sb[:, :], in_=d_idbf[:, :]),
       inc=s_cst, amt=16)
    op("vector", lambda e: e.memset(ones128[:, :], 1.0), inc=s_selfV)
    op("vector", lambda e: e.memset(ones1w[:, :], 1.0), inc=s_selfV)
    op("vector", lambda e: e.memset(onesK[:, :], 1.0), inc=s_selfV)
    op("vector", lambda e: e.memset(eps_sb[:, :], EPS), inc=s_selfV)
    cst_v = s_selfV.n
    op("scalar", load_table(TBL_LNEXP))

    def emit_a_loads(chx, queue):
        sems = s_ld if queue == "sync" else s_ldg
        base = [x.n for x in sems]
        for g in range(G):
            par = g % 8
            waits = []
            if g >= 8:
                waits.append((s_actA, 32 * chx + g - 7))
            if chx > 0 and g < 8:
                waits.append((s_peCb, chx * (NSB + 2)))
                waits.append((s_dveDa, chx * G))
            def f(e, g=g, par=par, chx=chx):
                return e.dma_start(out=x0Tbuf[par][:, :],
                                   in_=d_x0T[g * 128:(g + 1) * 128,
                                             chx * T:(chx + 1) * T])
            op(queue, f, waits=waits, inc=sems[par], amt=16)
        return (sems, base)

    def emit_a_casts(chx, lb, g0, g1):
        sems, lbase = lb
        for g in range(g0, g1):
            par = g % 8
            w2 = [(sems[par], lbase[par] + 16 * (g // 8 + 1))]
            if chx == 0 and g == 0:
                w2.append((s_cst, 64))
            if chx > 0:
                if g == 0:
                    w2.append((s_peU, chx * 128))
                elif g == 8:
                    w2.append((s_peDn, chx * 32))
                elif g == 20:
                    w2.append((s_peDb2, chx))
            if chx == 1 and g == 0:
                w2.append((s_udc, 16 * NH))
            def fc(e, g=g, par=par):
                return e.activation(out=x0bf[:, g * T:(g + 1) * T],
                                    in_=x0Tbuf[par][:, :], func=AF.Copy)
            op("scalar", fc, waits=w2, inc=s_actA)

    b_w_base_prev = None
    a_ld_base_next = None
    for ch in range(NCH):
        ch_tok = ch * T

        if ch == 0:
            a_ld_base = emit_a_loads(0, "sync")
        else:
            a_ld_base = a_ld_base_next
        a_cast_base = 32 * ch
        if ch == 0:
            emit_a_casts(0, a_ld_base, 0, G)

        b_pe_base = s_peB.n
        b_peh_base = s_peBh.n
        b_act_base = s_actB.n
        s_wBc = s_wBg if ch == 0 else s_wB
        b_w_base = ([s_wBc[0].n, s_wBc[1].n, s_wBc[2].n] if ch == 0
                    else b_w_base_prev)
        b_st_base = [s_stB[0].n, s_stB[1].n, s_stB[2].n]
        for t in range(3):
            for g in range(G):
                idx = t * G + g
                for half in range(2):
                    hidx = idx * 2 + half
                    par = hidx % 3
                    waits = []
                    if hidx >= 3:
                        pj = hidx - 3
                        if pj % 2 == 0:
                            waits.append((s_peBh, b_peh_base + pj // 2 + 1))
                        else:
                            waits.append((s_peB, b_pe_base + pj // 2 + 1))
                    elif ch > 0:
                        waits.append((s_peB, b_pe_base))
                        if ch == 1:
                            waits.append((s_stB[par], 16 * 64))
                    if ch == 0:
                        if hidx >= 3:
                            waits.append((s_stB[par],
                                          b_st_base[par] + 16 * (hidx // 3)))
                        def fw(e, t=t, g=g, half=half, par=par):
                            return e.dma_start(
                                out=wbuf[par][:, :],
                                in_=d_wT[t, g, :, half * 2048:(half + 1) * 2048])
                        op("gpsimd", fw, waits=waits, inc=s_wBg[par], amt=16)
                        def fs(e, t=t, g=g, half=half, par=par):
                            return e.dma_start(
                                out=d_wst[t, g, :, half * 2048:(half + 1) * 2048],
                                in_=wbuf[par][:, :])
                        op("sync", fs,
                           waits=[(s_wBg[par], b_w_base[par] + 16 * (hidx // 3 + 1))],
                           inc=s_stB[par], amt=16)
                    elif hidx >= 3:
                        def fw(e, t=t, g=g, half=half, par=par):
                            return e.dma_start(
                                out=wbuf[par][:, :],
                                in_=d_wst[t, g, :, half * 2048:(half + 1) * 2048])
                        op("sync", fw, waits=waits, inc=s_wB[par], amt=16)
                pbank = pb[idx % 2]
                for s in range(G):
                    hidx = idx * 2 + s // 16
                    par = hidx % 3
                    waits = []
                    if s == 0:
                        if idx >= 2:
                            waits.append((s_actB, b_act_base + idx - 1))
                        elif ch > 0:
                            waits.append((s_actE, ch * 128))
                    if idx < 8:
                        waits.append((s_actA, a_cast_base + s + 1))
                    elif s == 0:
                        waits.append((s_actA, a_cast_base + G))
                    if s % 16 == 0:
                        waits.append((s_wBc[par],
                                      b_w_base[par] + 16 * (hidx // 3 + 1)))
                    def fm(e, s=s, par=par, pbank=pbank):
                        return e.matmul(
                            pbank[:, :],
                            wbuf[par][:, (s % 16) * 128:(s % 16 + 1) * 128],
                            x0bf[:, s * T:(s + 1) * T],
                            start=(s == 0), stop=(s == G - 1))
                    inc = s_peB if s == G - 1 else (s_peBh if s == 15 else None)
                    op("tensor", fm, waits=waits, inc=inc)
                plane = (qT, kT, vT)[t]
                waits = [(s_peB, b_pe_base + idx + 1)]
                if idx == 0 and ch == 0:
                    waits.append((s_cst, 64))
                if t == 0 and g == 0 and ch > 0:
                    waits.append((s_x1, ch * 64))
                if t == 2 and g == 0 and ch > 0:
                    waits.append((s_out[0], ch * 32))
                    waits.append((s_out[1], ch * 32))
                def fe(e, t=t, g=g, plane=plane, pbank=pbank):
                    return e.activation(
                        out=plane[:, g:g + (T - 1) * G + 1:G],
                        in_=pbank[:, :], func=AF.Identity,
                        bias=b_sb2[:, t * G + g:t * G + g + 1], scale=1.0)
                op("scalar", fe, waits=waits, inc=s_actB)

        e_wf_base_early = [s_wF[0].n, s_wF[1].n]
        e_wd_base_early = [s_wD[0].n, s_wD[1].n]
        for hh in (0, 1):
            waits_u = [(s_udc, 16 * NH), (s_peB, s_peB.n)]
            waits_d = [(s_udc, 16 * NH), (s_peB, s_peB.n)]
            if ch > 0:
                waits_u.append((s_peU, ch * 128))
                waits_d.append((s_peDn, ch * 32))
            def fwfu(e, hh=hh):
                return e.dma_start(out=ubuf[hh][:, :],
                                   in_=d_ust[hh, :, 0:2048])
            op("sync", fwfu, waits=waits_u, inc=s_wF[hh], amt=16)
            def fwfd(e, hh=hh):
                return e.dma_start(out=dbuf[hh][:, :],
                                   in_=d_ust[hh, :, 2048:4096])
            op("sync", fwfd, waits=waits_d, inc=s_wD[hh], amt=16)

        c_pea_base = s_peCa.n
        c_peb_base = s_peCb.n
        c_acta_base = s_actCa.n
        c_dvev_base = s_dvev.n
        c_dver_base = s_dver.n
        c_dvee_base = s_dvee.n
        c_dvem_base = s_dveCm.n

        for i in (0, 1):
            def fz(e, i=i):
                return e.memset(probs_bd[i][:, :], 0.0)
            op("vector", fz, waits=[(s_peB, b_pe_base + 96)] if i == 0 else (),
               inc=s_dveCm)

        for k in range(NSB + 2):
            if k < NSB:
                waits = [(s_actB, b_act_base + 96)]
                if k == 0 and ch == 0:
                    waits.append((s_cst, 64))
                if k >= 2:
                    waits.append((s_actCa, c_acta_base + k - 1))
                    waits.append((s_dvev, c_dvev_base + k - 1))
                for j2 in range(4):
                    tok0 = k * 16 + j2 * 4
                    def ft(e, k=k, j2=j2, tok0=tok0):
                        return e.transpose(
                            pv[k % 2][:, j2 * 128:(j2 + 1) * 128],
                            vT[:, tok0 * G:(tok0 + 4) * G],
                            idbf_sb[:, :])
                    op("tensor", ft, waits=waits if j2 == 0 else ())
                for j2 in range(4):
                    for j in range(4):
                        tok = k * 16 + j2 * 4 + j
                        def fsc(e, k=k, j2=j2, j=j, tok=tok):
                            return e.matmul(
                                ps[k % 2][j * 32:(j + 1) * 32,
                                          j2 * 32:(j2 + 1) * 32],
                                kT[:, tok * G:tok * G + G],
                                qT[:, tok * G:tok * G + G],
                                start=True, stop=True,
                                tile_position=(0, j * 32))
                        op("tensor", fsc)
                op("tensor", lambda e: None, inc=s_peCa, drain=True)
            m = k - 1
            if 0 <= m < NSB:
                waits = [(s_actCa, c_acta_base + m + 1)]
                if m >= 2:
                    waits.append((s_dver, c_dver_base + m - 1))
                else:
                    waits.append((s_dveE, ch * 32))
                if ch == 0 and m == 0:
                    waits.append((s_selfV, cst_v))
                def fsum(e, m=m):
                    return e.matmul(pb[2 + m % 2][:, :], onesK[:, :],
                                    probs_bd[m % 2][:, :], start=True, stop=True)
                op("tensor", fsum, waits=waits)
                for j2 in range(4):
                    waits2 = []
                    if j2 == 0:
                        waits2.append((s_dvee, c_dvee_base + max(m - 1, 0)))
                        waits2.append((s_dvev, c_dvev_base + m + 1))
                    def fctx(e, m=m, j2=j2):
                        return e.matmul(
                            pb[4 + m % 2][:, j2 * 128:(j2 + 1) * 128],
                            vtok_sb[m % 2][:, j2 * 128:(j2 + 1) * 128],
                            probs_bd[m % 2][:, j2 * 128:(j2 + 1) * 128],
                            start=True, stop=True)
                    op("tensor", fctx, waits=waits2)
            op("tensor", lambda e: None, inc=s_peCb, drain=True)

        if ch == 0:
            for h in range(NH):
                def fudc(e, h=h):
                    return e.dma_start(out=d_ust[h, :, :], in_=d_updn[h, :, :])
                op("gpsimd", fudc, inc=s_udc, amt=16)

        op("scalar", load_table(TBL_LNEXP))
        for k in range(NSB + 1):
            if k < NSB:
                waits = [(s_peCa, c_pea_base + k + 1)]
                if k < 2:
                    waits.append((s_dveCm, c_dvem_base + 2))
                for j in range(4):
                    def fe2(e, k=k, j=j):
                        pr_r = probs_bd[k % 2].rearrange("p (j2 c) -> p j2 c",
                                                         j2=4)
                        ps_r = ps[k % 2][:, 0:128].rearrange(
                            "p (j2 h) -> p j2 h", j2=4)
                        return e.activation(
                            out=pr_r[j * 32:(j + 1) * 32, :, j * 32:(j + 1) * 32],
                            in_=ps_r[j * 32:(j + 1) * 32, :, :],
                            func=AF.Exp, scale=RSQ)
                    op("scalar", fe2, waits=waits if j == 0 else (),
                       inc=s_actCa if j == 3 else None)
            m = k - 1
            if 0 <= m < NSB:
                waits = [(s_peCb, c_peb_base + m + 2),
                         (s_dvee, c_dvee_base + max(m - 1, 0))]
                def fln(e, m=m):
                    return e.activation(out=rb_sb[m % 2][:, :],
                                        in_=pb[2 + m % 2][:, :], func=AF.Ln)
                op("scalar", fln, waits=waits, inc=s_selfD)
                svl = s_selfD.n
                def fex(e, m=m):
                    return e.activation(out=rb_sb[m % 2][:, :],
                                        in_=rb_sb[m % 2][:, :], func=AF.Exp,
                                        scale=-1.0)
                op("scalar", fex, waits=[(s_selfD, svl)], inc=s_dver)
        for k in range(NSB + 2):
            if k < NSB:
                def fv(e, k=k):
                    return e.tensor_copy(out=vtok_sb[k % 2][:, :],
                                         in_=pv[k % 2][:, :])
                op("vector", fv, waits=[(s_peCa, c_pea_base + k + 1)],
                   inc=s_dvev)
            m2 = k - 2
            if 0 <= m2 < NSB:
                for j2 in range(4):
                    waits = [(s_peCb, c_peb_base + k)]
                    if j2 == 0:
                        waits.append((s_dver, c_dver_base + m2 + 1))
                    def fcm(e, m2=m2, j2=j2):
                        tok0 = m2 * 16 + j2 * 4
                        dst = sumT.rearrange("p (h tok) -> p tok h", h=G)[
                            :, tok0:tok0 + 4, :]
                        src = pb[4 + m2 % 2][:, j2 * 128:(j2 + 1) * 128].rearrange(
                            "p (j h) -> p j h", j=4)
                        rbs = rb_sb[m2 % 2][:, j2 * 128:(j2 + 1) * 128].rearrange(
                            "p (j h) -> p j h", j=4)
                        return e.tensor_tensor(out=dst, in0=rbs, in1=src,
                                               op=OP.mult)
                    last = (j2 == 3)
                    op("vector", fcm, waits=waits,
                       inc=s_dvee if last else None)
        d_pe_base = s_peD.n
        d_as_base = s_actDs.n
        d_da_base = s_dveDa.n
        d_dc_base = s_dveDc.n
        d_dn_base = s_dveDn.n
        d_ax_base = s_actDx.n
        d_xd_base = [x.n for x in s_xd]
        for g in range(G):
            par4 = g % 8
            par2 = g % 2
            waits = []
            if g >= 8:
                waits.append((s_dveDa, d_da_base + g - 7))
            if g < 8:
                waits.append((s_actA, a_cast_base + G))
                waits.append((s_peCb, c_peb_base + NSB + 2))
            def fxl(e, g=g, par4=par4, ch_tok=ch_tok):
                return e.dma_start(out=x0Tbuf[par4][:, :],
                                   in_=d_x0T[g * 128:(g + 1) * 128,
                                             ch_tok:ch_tok + T])
            op("sync", fxl, waits=waits, inc=s_xd[par4], amt=16)
            waits = [(s_xd[par4], d_xd_base[par4] + 16 * (g // 8 + 1))]
            if g == 0:
                waits.append((s_dvee, c_dvee_base + NSB))
            def fadd(e, g=g, par4=par4):
                return e.tensor_tensor(out=sumT[:, g * T:(g + 1) * T],
                                       in0=sumT[:, g * T:(g + 1) * T],
                                       in1=x0Tbuf[par4][:, :], op=OP.add)
            op("vector", fadd, waits=waits, inc=s_dveDa)
            sda = s_dveDa.n
            waits = [(s_dveDa, sda)]
            if g >= 2:
                waits.append((s_peD, d_pe_base + g - 1))
            def fcast(e, g=g, par2=par2):
                return e.tensor_copy(out=xbf_scr[par2][:, :],
                                     in_=sumT[:, g * T:(g + 1) * T])
            op("vector", fcast, waits=waits, inc=s_dveDc)
            waits = [(s_dveDa, sda)]
            if g >= 2:
                waits.append((s_peD, d_pe_base + g - 1))
            def fsq(e, g=g, par2=par2):
                return e.activation(out=sq_scr[par2][:, :],
                                    in_=sumT[:, g * T:(g + 1) * T],
                                    func=AF.Square)
            op("scalar", fsq, waits=waits, inc=s_actDs)
            waits = [(s_dveDc, d_dc_base + g + 1), (s_actDs, d_as_base + g + 1)]
            if g == 0:
                waits.append((s_dver, c_dver_base + NSB))
                if ch == 0:
                    waits.append((s_selfV, cst_v))
            def fm1(e, g=g, par2=par2):
                return e.matmul(pb[2][0:1, :], ones128[:, :], xbf_scr[par2][:, :],
                                start=(g == 0), stop=(g == G - 1))
            op("tensor", fm1, waits=waits)
            def fm2(e, g=g, par2=par2):
                return e.matmul(pb[3][0:1, :], ones128[:, :], sq_scr[par2][:, :],
                                start=(g == 0), stop=(g == G - 1))
            op("tensor", fm2, inc=s_peD)
        def fr1(e):
            return e.tensor_scalar(out=row_mean[:, :], in0=pb[2][0:1, :],
                                   scalar1=1.0 / 4096.0, scalar2=None, op0=OP.mult)
        op("vector", fr1, waits=[(s_peD, d_pe_base + G)], inc=s_selfV)
        def fr2(e):
            return e.tensor_scalar(out=row_var[:, :], in0=pb[3][0:1, :],
                                   scalar1=1.0 / 4096.0, scalar2=None, op0=OP.mult)
        op("vector", fr2, inc=s_selfV)
        sd2 = s_selfV.n
        def fr3(e):
            return e.tensor_tensor(out=row_A[:, :], in0=row_mean[:, :],
                                   in1=row_mean[:, :], op=OP.mult)
        op("vector", fr3, waits=[(s_selfV, sd2)], inc=s_selfV)
        sd3 = s_selfV.n
        def fr4(e):
            return e.tensor_tensor(out=row_var[:, :], in0=row_var[:, :],
                                   in1=row_A[:, :], op=OP.subtract)
        op("vector", fr4, waits=[(s_selfV, sd3)], inc=s_selfV)
        sd4 = s_selfV.n
        def fr5(e):
            return e.activation(out=row_B[:, :], in_=row_var[:, :], func=AF.Ln,
                                bias=eps_sb[0:1, 0:1], scale=1.0)
        op("scalar", fr5, waits=[(s_selfV, sd4)], inc=s_selfD)
        sd5 = s_selfD.n
        def fr6(e):
            return e.activation(out=row_A[:, :], in_=row_B[:, :], func=AF.Exp,
                                scale=-0.5)
        op("scalar", fr6, waits=[(s_selfD, sd5)], inc=s_selfD)
        sd6 = s_selfD.n
        def fr7(e):
            return e.tensor_tensor(out=row_B[:, :], in0=row_mean[:, :],
                                   in1=row_A[:, :], op=OP.mult)
        op("vector", fr7, waits=[(s_selfD, sd6)], inc=s_selfV)
        sd7 = s_selfV.n
        def fr8(e):
            return e.tensor_scalar(out=row_B[:, :], in0=row_B[:, :],
                                   scalar1=-1.0, scalar2=None, op0=OP.mult)
        op("vector", fr8, waits=[(s_selfV, sd7)], inc=s_selfV)
        sd8 = s_selfV.n
        def fhiA(e):
            return e.tensor_copy(out=row_hi[:, :], in_=row_A[:, :])
        op("vector", fhiA, waits=[(s_selfV, sd8)], inc=s_selfV)
        sd9 = s_selfV.n
        def floA(e):
            return e.tensor_tensor(out=row_lo[:, :], in0=row_A[:, :],
                                   in1=row_hi[:, :], op=OP.subtract)
        op("vector", floA, waits=[(s_selfV, sd9)], inc=s_selfV)
        sd10 = s_selfV.n
        def fbA1(e):
            return e.matmul(pb[0][:, :], ones1w[:, :], row_hi[:, :],
                            start=True, stop=False)
        op("tensor", fbA1, waits=[(s_selfV, sd10),
                                  (s_dvee, c_dvee_base + NSB)])
        def fbA2(e):
            return e.matmul(pb[0][:, :], ones1w[:, :], row_lo[:, :],
                            start=False, stop=True)
        op("tensor", fbA2, inc=s_peDb1, drain=True)
        db1 = s_peDb1.n
        def fhiB(e):
            return e.tensor_copy(out=row_hi[:, :], in_=row_B[:, :])
        op("vector", fhiB, waits=[(s_peDb1, db1)], inc=s_selfV)
        sd11 = s_selfV.n
        def floB(e):
            return e.tensor_tensor(out=row_lo[:, :], in0=row_B[:, :],
                                   in1=row_hi[:, :], op=OP.subtract)
        op("vector", floB, waits=[(s_selfV, sd11)], inc=s_selfV)
        sd12 = s_selfV.n
        def fbB1(e):
            return e.matmul(pb[1][:, :], ones1w[:, :], row_hi[:, :],
                            start=True, stop=False)
        op("tensor", fbB1, waits=[(s_selfV, sd12)])
        def fbB2(e):
            return e.matmul(pb[1][:, :], ones1w[:, :], row_lo[:, :],
                            start=False, stop=True)
        op("tensor", fbB2, inc=s_peDb2, drain=True)
        db2 = s_peDb2.n
        def fcb0(e):
            return e.tensor_copy(out=rb_sb[0][:, :], in_=pb[0][:, :])
        op("vector", fcb0, waits=[(s_peDb1, db1)], inc=s_selfV)
        def fcb1(e):
            return e.tensor_copy(out=rb_sb[1][:, :], in_=pb[1][:, :])
        op("vector", fcb1, waits=[(s_peDb2, db2)], inc=s_selfV)
        sd_cp = s_selfV.n
        gp_dn_base = s_gpDn.n
        n_dve = 0
        n_gps = 0
        for g in range(G):
            if g % 3 == 2:
                waits = [(s_selfV, sd_cp)] if n_gps == 0 else ()
                def fp1(e, g=g):
                    return e.tensor_tensor(out=sumT[:, g * T:(g + 1) * T],
                                           in0=sumT[:, g * T:(g + 1) * T],
                                           in1=rb_sb[0][:, :], op=OP.mult)
                op("gpsimd", fp1, waits=waits, inc=s_gpSelf)
                sgp = s_gpSelf.n
                def fp2(e, g=g):
                    return e.tensor_tensor(out=sumT[:, g * T:(g + 1) * T],
                                           in0=sumT[:, g * T:(g + 1) * T],
                                           in1=rb_sb[1][:, :], op=OP.add)
                op("gpsimd", fp2, waits=[(s_gpSelf, sgp)], inc=s_gpDn)
                n_gps += 1
                cast_wait = (s_gpDn, gp_dn_base + n_gps)
            else:
                waits = [(s_peDb2, db2)] if n_dve == 0 else ()
                def fn1(e, g=g):
                    return e.tensor_tensor(out=sumT[:, g * T:(g + 1) * T],
                                           in0=sumT[:, g * T:(g + 1) * T],
                                           in1=pb[0][:, :], op=OP.mult)
                op("vector", fn1, waits=waits, inc=s_selfV)
                sdg = s_selfV.n
                def fn2(e, g=g):
                    return e.tensor_tensor(out=sumT[:, g * T:(g + 1) * T],
                                           in0=sumT[:, g * T:(g + 1) * T],
                                           in1=pb[1][:, :], op=OP.add)
                op("vector", fn2, waits=[(s_selfV, sdg)], inc=s_dveDn)
                n_dve += 1
                cast_wait = (s_dveDn, d_dn_base + n_dve)
            def fxc(e, g=g, cast_wait=cast_wait):
                return e.activation(out=x1bf[:, g * T:(g + 1) * T],
                                    in_=sumT[:, g * T:(g + 1) * T],
                                    func=AF.Copy)
            op("scalar", fxc, waits=[cast_wait], inc=s_actDx)
        d_dn_total = s_dveDn.n
        gp_dn_total = s_gpDn.n
        d_x1v = d_x1bf.rearrange("(g p) tok -> p g tok", p=128)
        x1bf_v = x1bf.rearrange("p (g tok) -> p g tok", g=G)
        for piece in range(4):
            def fx1(e, piece=piece):
                return e.dma_start(
                    out=d_x1v[:, piece * 8:(piece + 1) * 8, :],
                    in_=x1bf_v[:, piece * 8:(piece + 1) * 8, :])
            op("sync", fx1, waits=[(s_actDx, d_ax_base + 8 * (piece + 1))],
               inc=s_x1, amt=16)

        s_wFc, s_wDc = s_wF, s_wD
        e_wf_base = e_wf_base_early
        e_wd_base = e_wd_base_early

        e_g_base = [x.n for x in s_g]
        e_peU_base = s_peU.n
        e_actE_base = s_actE.n
        e_peDn_base = s_peDn.n
        e_dveE_base = s_dveE.n
        e_sc_base = [x.n for x in s_sc]
        op("scalar", load_table(TBL_GELU))

        def emit_gathers(hh):
            for c2 in range(2):
                waits = [(s_x1, 16 * 4 * (ch + 1))]
                if hh == 0 and c2 == 0 and ch == 0:
                    waits.append((s_cst, 64))
                if hh >= 4:
                    waits.append((s_peU, e_peU_base + 8 * (hh - 3)))
                def fg(e, hh=hh, c2=c2):
                    return e.indirect_dma_start(
                        out=xg[hh % 4][:, c2 * T:(c2 + 1) * T], out_offset=None,
                        in_=d_x1bf[:, :],
                        in_offset=bass.IndirectOffsetOnAxis(
                            ap=perm_sb[:, 2 * hh + c2:2 * hh + c2 + 1], axis=0))
                op("gpsimd", fg, waits=waits, inc=s_g[hh % 4], amt=16)

        for hh in range(4):
            emit_gathers(hh)
        for h in range(NH):
            par = h % 2
            waits = [(s_udc, 16 * NH)]
            if h >= 2:
                waits.append((s_peU, e_peU_base + 8 * (h - 1)))
            else:
                waits.append((s_peB, b_pe_base + 96))
            if h >= 2:
                def fwfu(e, h=h, par=par):
                    return e.dma_start(out=ubuf[par][:, :], in_=d_ust[h, :, 0:2048])
                op("sync", fwfu, waits=waits, inc=s_wF[par], amt=16)
            waits = [(s_udc, 16 * NH)]
            if h >= 2:
                waits.append((s_peDn, e_peDn_base + 2 * (h - 1)))
            else:
                waits.append((s_peB, b_pe_base + 96))
            if h >= 2:
                def fwfd(e, h=h, par=par):
                    return e.dma_start(out=dbuf[par][:, :],
                                       in_=d_ust[h, :, 2048:4096])
                op("sync", fwfd, waits=waits, inc=s_wD[par], amt=16)
            if h + 4 < NH:
                emit_gathers(h + 4)
            for mt in range(8):
                ubank = (pb[0], pb[1], pb[4], pb[5])[mt % 4]
                waits = [(s_wFc[par], e_wf_base[par] + 16 * (h // 2 + 1)),
                         (s_g[h % 4], e_g_base[h % 4] + 32 * (h // 4 + 1))]
                if h * 8 + mt >= 4:
                    waits.append((s_actE, e_actE_base + h * 8 + mt - 3))
                elif mt < 2:
                    waits.append((s_dveDn, d_dn_total))
                    waits.append((s_gpDn, gp_dn_total))
                else:
                    waits.append((s_dvee, c_dvee_base + NSB))
                for ct in range(2):
                    def fu(e, h=h, mt=mt, ct=ct, par=par, ubank=ubank):
                        return e.matmul(
                            ubank[:, :],
                            ubuf[par][:, ct * 1024 + mt * 128:
                                      ct * 1024 + (mt + 1) * 128],
                            xg[h % 4][:, ct * T:(ct + 1) * T],
                            start=(ct == 0), stop=(ct == 1))
                    op("tensor", fu, waits=waits if ct == 0 else (),
                       inc=s_peU if ct == 1 else None, drain=(ct == 1))
                def fgel(e, mt=mt, par=par, ubank=ubank):
                    return e.activation(out=gelu_sb[par][:, mt * T:(mt + 1) * T],
                                        in_=ubank[:, :], func=AF.Gelu)
                op("scalar", fgel, waits=[(s_peU, e_peU_base + h * 8 + mt + 1)],
                   inc=s_actE)
            for dt in range(2):
                dbank = pb[2 + dt]
                waits = [(s_actE, e_actE_base + 8 * (h + 1)),
                         (s_wDc[par], e_wd_base[par] + 16 * (h // 2 + 1))]
                if h >= 1:
                    waits.append((s_dveE, e_dveE_base + 2 * (h - 1) + dt + 1))
                else:
                    waits.append((s_selfV, sd2))
                for mt in range(8):
                    def fd(e, dt=dt, mt=mt, par=par, dbank=dbank):
                        return e.matmul(
                            dbank[:, :],
                            dbuf[par][:, mt * 256 + dt * 128:
                                      mt * 256 + (dt + 1) * 128],
                            gelu_sb[par][:, mt * T:(mt + 1) * T],
                            start=(mt == 0), stop=(mt == 7))
                    op("tensor", fd, waits=waits if mt == 0 else (),
                       inc=s_peDn if mt == 7 else None, drain=(mt == 7))
                waits = [(s_peDn, e_peDn_base + 2 * h + dt + 1)]
                fb4 = (2 * h + dt) % 4
                if h >= 2:
                    waits.append((s_sc[fb4], e_sc_base[fb4] + 16 * (h // 2)))
                def fde(e, dt=dt, fb4=fb4, dbank=dbank):
                    return e.tensor_copy(out=ffn_st[fb4][:, :], in_=dbank[:, :])
                op("vector", fde, waits=waits, inc=s_dveE)
                def fsc2(e, h=h, dt=dt, fb4=fb4):
                    return e.indirect_dma_start(
                        out=d_ffnat[:, :],
                        out_offset=bass.IndirectOffsetOnAxis(
                            ap=perm_sb[:, 2 * h + dt:2 * h + dt + 1], axis=0),
                        in_=ffn_st[fb4][:, :], in_offset=None)
                op("gpsimd", fsc2,
                   waits=[(s_dveE, e_dveE_base + 2 * h + dt + 1)],
                   inc=s_sc[fb4], amt=16)

        b_w_base_prev = [s_wB[0].n, s_wB[1].n, s_wB[2].n]
        if ch + 1 < NCH:
            for hidx in range(3):
                par = hidx % 3
                t_n, g_n, half_n = hidx // 64, (hidx // 2) % G, hidx % 2
                waits = [(s_peB, 96 * (ch + 1))]
                if ch == 0:
                    waits.append((s_stB[par], 16 * 64))
                def fwe(e, t_n=t_n, g_n=g_n, half_n=half_n, par=par):
                    return e.dma_start(
                        out=wbuf[par][:, :],
                        in_=d_wst[t_n, g_n, :,
                                  half_n * 2048:(half_n + 1) * 2048])
                op("sync", fwe, waits=waits, inc=s_wB[par], amt=16)

        s6_fa_base = [x.n for x in s_fa]
        s6_p6_base = s_pe6.n
        s6_e_base = s_dve6e.n
        s6_n_base = s_act6n.n
        s6_out_base = [s_out[0].n, s_out[1].n]
        if ch + 1 < NCH:
            a_ld_base_next = emit_a_loads(ch + 1, "sync")
            emit_a_casts(ch + 1, a_ld_base_next, 0, G)
        op("scalar", load_table(TBL_SQRT))
        for k in range(G // 2):
            par = k % 3
            waits = [(s_sc[i], e_sc_base[i] + 16 * 8) for i in range(4)]
            if k >= 3:
                waits.append((s_fa[par], s6_fa_base[par] + 16 * (k // 3)))
            def ffb(e, k=k):
                return e.dma_start(
                    out=sumT.rearrange("p (g t) -> p g t", g=G)[
                        :, 2 * k:2 * k + 2, :],
                    in_=d_ffnat[256 * k:256 * (k + 1), :].rearrange(
                        "(b p) t -> p b t", p=128),
                    accum_op=OP.add)
            op("gpsimd", ffb, waits=waits, inc=s_fa[par], amt=16)
        for tt in range(4):
            for q in range(8):
                qidx = tt * 8 + q
                par = qidx % 2
                k0, k1 = 2 * q, 2 * q + 1
                waits = [(s_fa[k0 % 3], s6_fa_base[k0 % 3] + 16 * (k0 // 3 + 1)),
                         (s_fa[k1 % 3], s6_fa_base[k1 % 3] + 16 * (k1 // 3 + 1))]
                if qidx == 0:
                    waits.append((s_dveDn, d_dn_total))
                    waits.append((s_gpDn, gp_dn_total))
                    waits.append((s_actE, e_actE_base + 8 * NH))
                    if ch == 0:
                        waits.append((s_cst, 64))
                if qidx >= 2:
                    waits.append((s_dve6e, s6_e_base + qidx - 1))
                for gg in range(4):
                    g = q * 4 + gg
                    def ftr(e, tt=tt, g=g, gg=gg, par=par):
                        return e.transpose(
                            pb[6 + par][:, gg * 128:(gg + 1) * 128],
                            sumT[:, g * T + tt * 128:g * T + (tt + 1) * 128],
                            id32_sb[:, :])
                    op("tensor", ftr, waits=waits if gg == 0 else ())
                op("tensor", lambda e: None, inc=s_pe6, drain=True)
                waits = [(s_pe6, s6_p6_base + qidx + 1)]
                if q == 0:
                    if tt >= 2:
                        waits.append((s_out[tt % 2],
                                      s6_out_base[tt % 2] + 16 * (tt // 2)))
                    else:
                        for i in range(4):
                            waits.append((s_sc[i], e_sc_base[i] + 16 * 8))
                def fev(e, tt=tt, q=q, par=par):
                    return e.tensor_copy(
                        out=out_st[tt % 2][:, q * 512:(q + 1) * 512],
                        in_=pb[6 + par][:, :])
                op("vector", fev, waits=waits, inc=s_dve6e)
            for jj in range(8):
                def fbn1(e, tt=tt, jj=jj):
                    return e.bn_stats(out=s6_bn[:, jj * 6:(jj + 1) * 6],
                                      in_=out_st[tt % 2][:, jj * 512:(jj + 1) * 512])
                op("vector", fbn1,
                   waits=[(s_dve6e, s6_e_base + (tt + 1) * 8)] if jj == 0 else (),
                   inc=s_self6)
            sb8 = s_self6.n
            def fagg(e, tt=tt):
                return e.bn_aggr(out=s6_mv[:, :], in_=s6_bn[:, :])
            op("vector", fagg, waits=[(s_self6, sb8)], inc=s_self6)
            sb9 = s_self6.n
            def fstd(e, tt=tt):
                return e.activation(out=s6_std[:, :], in_=s6_mv[:, 1:2],
                                    func=AF.Sqrt, bias=eps_sb[:, 0:1], scale=1.0)
            op("scalar", fstd, waits=[(s_self6, sb9)], inc=s_self6)
            sb10 = s_self6.n
            def frst(e, tt=tt):
                return e.reciprocal(out=s6_rstd[:, :], in_=s6_std[:, :])
            op("vector", frst, waits=[(s_self6, sb10)], inc=s_self6)
            sb11 = s_self6.n
            def fnm(e, tt=tt):
                return e.tensor_tensor(out=s6_nmr[:, :], in0=s6_mv[:, 0:1],
                                       in1=s6_rstd[:, :], op=OP.mult)
            op("vector", fnm, waits=[(s_self6, sb11)], inc=s_self6)
            sb12 = s_self6.n
            def fng(e, tt=tt):
                return e.tensor_scalar(out=s6_nmr[:, :], in0=s6_nmr[:, :],
                                       scalar1=-1.0, scalar2=None, op0=OP.mult)
            op("vector", fng, waits=[(s_self6, sb12)], inc=s_self6)
            sb13 = s_self6.n
            def fnorm(e, tt=tt):
                return e.activation(out=out_st[tt % 2][:, :],
                                    in_=out_st[tt % 2][:, :], func=AF.Identity,
                                    bias=s6_nmr[:, 0:1], scale=s6_rstd[:, 0:1])
            op("scalar", fnorm, waits=[(s_self6, sb13)], inc=s_act6n)
            def fout(e, tt=tt, ch=ch):
                return e.dma_start(
                    out=d_out[(ch * 4 + tt) * 128:(ch * 4 + tt + 1) * 128, :],
                    in_=out_st[tt % 2][:, :])
            op("sync", fout, waits=[(s_act6n, s6_n_base + tt + 1)],
               inc=s_out[tt % 2], amt=16)

    op("sync", lambda e: e.nop(),
       waits=[(s_out[0], s_out[0].n), (s_out[1], s_out[1].n)])

    blk = nc.Block().__enter__()

    import bass_rust

    def emit_engine(eng, name):
        evented = set()
        for o in OPL[name]:
            for (sem, val) in o.waits:
                eng.wait_ge(sem.h, val)
            instr = o.fn(eng)
            if o.incs:
                if instr is None:
                    raw = nc.cur_bb.bb.instructions[-1]
                    if raw.name in evented:
                        for (sem, amt, tgt) in o.incs:
                            eng.wait_ge(sem.h, tgt - amt)
                            eng.sem_inc(sem.h, amt)
                        continue
                    evented.add(raw.name)
                    instr = bass_rust.BassInstruction(raw)
                else:
                    try:
                        evented.add(instr.ins.name)
                    except Exception:
                        pass
                for (sem, amt, tgt) in o.incs:
                    instr.then_inc(sem.h, amt)

    @blk.sync
    def _(e):
        emit_engine(e, "sync")

    @blk.tensor
    def _(e):
        emit_engine(e, "tensor")

    @blk.scalar
    def _(e):
        emit_engine(e, "scalar")

    @blk.vector
    def _(e):
        emit_engine(e, "vector")

    @blk.gpsimd
    def _(e):
        emit_engine(e, "gpsimd")

    blk.__exit__(None, None, None)
    return nc


# version 26
# speedup vs baseline: 1.1208x; 1.0601x over previous
import numpy as np
import concourse.bass as bass
import concourse.mybir as mybir
from concourse.alu_op_type import AluOpType
from concourse.bass_utils import run_bass_kernel_spmd
from concourse.hw_specs import get_activation_tables

F32 = mybir.dt.float32
BF16 = mybir.dt.bfloat16
U32 = mybir.dt.uint32
AF = mybir.ActivationFunctionType
OP = AluOpType

NCORES = 8
NTOK = 2048
T = 512
NCH = NTOK // T
G = 32
NSB = T // 16
NH = 16
EPS = 1e-5
RSQ = float(1.0 / np.sqrt(128.0))


class Sem:
    def __init__(self, nc, name):
        self.h = nc.semaphore(name).__enter__()
        self.n = 0

    def bump(self, amt):
        self.n += amt
        return self.n


class Op:
    __slots__ = ("waits", "fn", "incs", "drain")

    def __init__(self, waits, fn, incs, drain):
        self.waits, self.fn, self.incs, self.drain = waits, fn, incs, drain


def kernel(L0_attn_hidden_states, L0_attn_wq, L0_attn_bq, L0_attn_wk, L0_attn_bk,
           L0_attn_wv, L0_attn_bv, L0_attn_wl, L0_attn_bl,
           L0_ffn_up, L0_ffn_down, L0_ffn_perm):
    import ml_dtypes
    x = np.ascontiguousarray(np.asarray(L0_attn_hidden_states, dtype=np.float32)
                             .reshape(NCORES * NTOK, 4096))

    def blockT(w):
        w4 = np.asarray(w, np.float32).reshape(G, 128, G, 128)
        return np.ascontiguousarray(w4.transpose(0, 3, 2, 1).reshape(G, 128, 4096))

    wT = np.ascontiguousarray(np.stack(
        [blockT(L0_attn_wq), blockT(L0_attn_wk), blockT(L0_attn_wv)]))
    bias = np.ascontiguousarray(
        np.stack([np.ascontiguousarray(np.asarray(b, np.float32).reshape(G, 128).T)
                  for b in (L0_attn_bq, L0_attn_bk, L0_attn_bv)]))
    up5 = np.asarray(L0_ffn_up, np.float32).reshape(NH, 2, 128, 1024) \
        .transpose(0, 2, 1, 3).reshape(NH, 128, 2048)
    dn5 = np.asarray(L0_ffn_down, np.float32).reshape(NH, 8, 128, 256) \
        .transpose(0, 2, 1, 3).reshape(NH, 128, 2048)
    updn = np.ascontiguousarray(np.concatenate([up5, dn5], axis=2))
    permT = np.ascontiguousarray(
        np.asarray(L0_ffn_perm).astype(np.int64).astype(np.uint32)
        .reshape(G, 128).T)
    id32 = np.eye(128, dtype=np.float32)
    idbf = np.eye(128, dtype=ml_dtypes.bfloat16)

    nc = build_program()

    in_maps = []
    for i in range(NCORES):
        xT = np.ascontiguousarray(x[i * NTOK:(i + 1) * NTOK].T)
        in_maps.append({"x0T": xT, "wT": wT, "bias": bias, "updn": updn,
                        "permT": permT, "id32": id32, "idbf": idbf})
    res = run_bass_kernel_spmd(nc, in_maps, core_ids=list(range(NCORES)))
    out = np.concatenate([r["out"] for r in res.results], axis=0)
    return out.reshape(2, 8192, 4096)


def build_program():
    nc = bass.Bass("TRN2", target_bir_lowering=False)
    OPL = {k: [] for k in ("sync", "tensor", "scalar", "vector", "gpsimd")}

    d_x0T = nc.dram_tensor("x0T", [4096, NTOK], F32, kind="ExternalInput")
    d_wT = nc.dram_tensor("wT", [3, G, 128, 4096], F32, kind="ExternalInput")
    d_bias = nc.dram_tensor("bias", [3, 128, G], F32, kind="ExternalInput")
    d_updn = nc.dram_tensor("updn", [NH, 128, 4096], F32, kind="ExternalInput")
    d_perm = nc.dram_tensor("permT", [128, G], U32, kind="ExternalInput")
    d_id32 = nc.dram_tensor("id32", [128, 128], F32, kind="ExternalInput")
    d_idbf = nc.dram_tensor("idbf", [128, 128], BF16, kind="ExternalInput")
    d_out = nc.dram_tensor("out", [NTOK, 4096], F32, kind="ExternalOutput")
    d_wst = nc.dram_tensor("wstash", [3, G, 128, 4096], BF16, kind="Internal")
    d_ust = nc.dram_tensor("updnstash", [NH, 128, 4096], BF16, kind="Internal")
    d_x1bf = nc.dram_tensor("x1bf", [4096, T], BF16, kind="Internal")
    d_ffnat = nc.dram_tensor("ffnat", [4096, T], F32, kind="Internal")

    arena_items = []
    arena_size = [0]

    def sb_at(off, name, shape, dtype):
        import functools, operator
        per_part = functools.reduce(operator.mul, shape[1:], 1) * mybir.dt.size(dtype)
        assert off % 32 == 0, name
        arena_size[0] = max(arena_size[0], off + per_part)
        h = nc.virtual_sbuf_tensor(name, shape, dtype)
        arena_items.append((h, off))
        return h

    sumT = sb_at(0, "sumT", [128, G * T], F32)
    qT = sb_at(65536, "qT", [128, G * T], BF16)
    kT = sb_at(98304, "kT", [128, G * T], BF16)
    x1bf = sb_at(65536, "x1bfp", [128, G * T], BF16)
    xbf_scr = [sb_at(98304 + i * 1024, f"xbfs{i}", [128, T], BF16) for i in (0, 1)]
    sq_scr = [sb_at(100352 + i * 1024, f"sqs{i}", [128, T], BF16) for i in (0, 1)]
    x0Tbuf = [sb_at(102400 + i * 2048, f"x0Tb{i}", [128, T], F32)
              for i in range(8)]
    vT = sb_at(131072, "vT", [128, G * T], BF16)
    gelu_sb = [sb_at(131072 + i * 8192, f"gelu{i}", [128, 8 * T], BF16)
               for i in (0, 1)]
    xg = [sb_at(147456 + i * 2048, f"xg{i}", [128, 2 * T], BF16)
          for i in (0, 1)] + [sb_at(155648 + i * 2048, f"xg{2+i}", [128, 2 * T],
                                    BF16) for i in (0, 1)]
    ffn_st = [sb_at(151552 + i * 2048, f"ffnst{i}", [128, T], F32)
              for i in (0, 1)] + [sb_at(159744 + i * 2048, f"ffnst{2+i}",
                                        [128, T], F32) for i in (0, 1)]
    out_st = [sb_at(131072 + i * 16384, f"outst{i}", [128, 4096], F32)
              for i in (0, 1)]
    x0bf = sb_at(163840, "x0bf", [128, G * T], BF16)
    ubuf = [sb_at(163840 + i * 4096, f"ubuf{i}", [128, 2048], BF16)
            for i in (0, 1)]
    dbuf = [sb_at(172032 + i * 4096, f"dbuf{i}", [128, 2048], BF16)
            for i in (0, 1)]
    fbuf = [sb_at(172032 + i * 4096, f"fbuf{i}", [128, 2 * T], F32)
            for i in range(3)]
    probs_bd = [sb_at(184320 + i * 1024, f"probsbd{i}", [128, 4 * 128], BF16)
                for i in (0, 1)]
    vtok_sb = [sb_at(186368 + i * 1024, f"vtok{i}", [128, 4 * 128], BF16)
               for i in (0, 1)]
    rb_sb = [sb_at(188416, "rb0", [128, T], F32),
             sb_at(190464, "rb1", [128, T], F32)]
    onesK = sb_at(210816, "onesK", [128, 128], BF16)
    row_mean = sb_at(184320, "rowmean", [1, T], F32)
    row_var = sb_at(186368, "rowvar", [1, T], F32)
    row_A = sb_at(188416, "rowA", [1, T], F32)
    row_B = sb_at(190464, "rowB", [1, T], F32)
    row_hi = sb_at(192512, "rowhi", [1, T], BF16)
    row_lo = sb_at(193536, "rowlo", [1, T], BF16)
    wbuf = [sb_at(196608 + i * 4096, f"wbuf{i}", [128, 2048], BF16)
            for i in (0, 1, 2)]
    b_sb2 = sb_at(208896, "b_sb2", [128, 3 * G], F32)
    perm_sb = sb_at(209280, "perm_sb", [128, G], U32)
    id32_sb = sb_at(209408, "id32_sb", [128, 128], F32)
    idbf_sb = sb_at(209920, "idbf_sb", [128, 128], BF16)
    ones128 = sb_at(210176, "ones128", [128, 1], BF16)
    ones1w = sb_at(210208, "ones1w", [1, 128], BF16)
    s6_std = sb_at(210464, "s6_std", [128, 1], F32)
    s6_rstd = sb_at(210496, "s6_rstd", [128, 1], F32)
    s6_nmr = sb_at(210528, "s6_nmr", [128, 1], F32)
    s6_bn = sb_at(210560, "s6_bn", [128, 48], F32)
    s6_mv = sb_at(210752, "s6_mv", [128, 2], F32)
    eps_sb = sb_at(210784, "eps_sb", [128, 1], F32)
    assert arena_size[0] <= 212800, arena_size[0]

    backing = nc.sbuf_tensor("arena", [128, arena_size[0]],
                             mybir.dt.uint8).__enter__()
    base_addr = nc.lookup_mloc(backing).addr
    for h, off in arena_items:
        nc._tensor(h.name, h.shape, h.dtype, type="SB")
        m = nc.lookup_mloc(h)
        m.allocated = True
        m.addr = base_addr + off

    pb = [nc.psum_tensor(f"pb{i}", [128, 512], F32).__enter__() for i in range(8)]
    pv = []
    for i in (0, 1):
        mb = nc.lookup_mloc(pb[6 + i])
        h1 = nc.virtual_psum_tensor(f"pv{i}", [128, 512], BF16)
        nc._tensor(h1.name, h1.shape, h1.dtype, type="PSUM")
        m = nc.lookup_mloc(h1)
        m.allocated = True; m.addr = mb.addr + 0; m.bank = mb.bank
        pv.append(h1)
    ps = [pb[0], pb[1]]

    S = lambda n: Sem(nc, n)
    s_cst = S("cst")
    s_ld = [S(f"ld{i}") for i in range(8)]
    s_ldg = [S(f"ldg{i}") for i in range(8)]
    s_wB = [S("wB0"), S("wB1"), S("wB2")]
    s_wBg = [S("wBg0"), S("wBg1"), S("wBg2")]
    s_stB = [S("stB0"), S("stB1"), S("stB2")]
    s_wF = [S("wF0"), S("wF1")]
    s_wFg = [S("wFg0"), S("wFg1")]
    s_wD = [S("wD0"), S("wD1")]
    s_udc = S("udc")
    s_stF = [S("stF0"), S("stF1")]
    s_g = [S("g0"), S("g1"), S("g2"), S("g3")]
    s_xd = [S(f"xd{i}") for i in range(8)]
    s_fb = [S(f"fb{i}") for i in range(3)]
    s_fa = [S(f"fa{i}") for i in range(3)]
    s_sc = [S("sc0"), S("sc1"), S("sc2"), S("sc3")]
    s_out = [S("out0"), S("out1")]
    s_x1 = S("x1")
    s_peB, s_peBh, s_actB, s_actA = S("peB"), S("peBh"), S("actB"), S("actA")
    s_peCa, s_peCb = S("peCa"), S("peCb")
    s_actCa, s_dvev, s_dveCm = S("actCa"), S("dvev"), S("dveCm")
    s_dver, s_dvee = S("dver"), S("dvee")
    s_peD, s_peDb1, s_peDb2 = S("peD"), S("peDb1"), S("peDb2")
    s_actDs, s_dveDa, s_dveDc, s_dveDn, s_actDx = \
        S("actDs"), S("dveDa"), S("dveDc"), S("dveDn"), S("actDx")
    s_peU, s_actE, s_peDn, s_dveE = S("peU"), S("actE"), S("peDn"), S("dveE")
    s_dve6, s_pe6, s_dve6e, s_act6n = S("dve6"), S("pe6"), S("dve6e"), S("act6n")
    s_selfD, s_selfV, s_self6 = S("selfD"), S("selfV"), S("self6")

    tables = list(get_activation_tables(nc.m.arch).keys())
    TBL_LNEXP = tables.index("natural_log_exp_and_others")
    TBL_GELU = tables.index("gelu_and_others")
    TBL_SQRT = tables.index("sqrt_and_others")

    def load_table(idx):
        def f(e):
            return e.add_instruction(mybir.InstLoadActFuncSet(
                name=nc.get_next_instruction_name(), act_func_set_id=idx,
                ins=[], outs=[]))
        return f

    def op(engine, fn, waits=(), inc=None, amt=1, drain=False):
        incs = []
        if inc is not None:
            inc.bump(amt)
            incs.append((inc, amt, inc.n))
        OPL[engine].append(Op([(s, v) for (s, v) in waits if v > 0],
                              fn, incs, drain))

    op("sync", lambda e: e.dma_start(
        out=b_sb2.rearrange("p (t g) -> p t g", t=3),
        in_=d_bias.rearrange("t p g -> p t g")),
        inc=s_cst, amt=16)
    op("sync", lambda e: e.dma_start(out=perm_sb[:, :], in_=d_perm[:, :]),
       inc=s_cst, amt=16)
    op("sync", lambda e: e.dma_start(out=id32_sb[:, :], in_=d_id32[:, :]),
       inc=s_cst, amt=16)
    op("sync", lambda e: e.dma_start(out=idbf_sb[:, :], in_=d_idbf[:, :]),
       inc=s_cst, amt=16)
    op("vector", lambda e: e.memset(ones128[:, :], 1.0), inc=s_selfV)
    op("vector", lambda e: e.memset(ones1w[:, :], 1.0), inc=s_selfV)
    op("vector", lambda e: e.memset(onesK[:, :], 1.0), inc=s_selfV)
    op("vector", lambda e: e.memset(eps_sb[:, :], EPS), inc=s_selfV)
    cst_v = s_selfV.n
    op("scalar", load_table(TBL_LNEXP))

    def emit_a_loads(chx, queue):
        sems = s_ld if queue == "sync" else s_ldg
        base = [x.n for x in sems]
        for g in range(G):
            par = g % 8
            waits = []
            if g >= 8:
                waits.append((s_actA, 32 * chx + g - 7))
            if chx > 0 and g < 8:
                waits.append((s_peCb, chx * (NSB + 2)))
                waits.append((s_dveDa, chx * G))
            def f(e, g=g, par=par, chx=chx):
                return e.dma_start(out=x0Tbuf[par][:, :],
                                   in_=d_x0T[g * 128:(g + 1) * 128,
                                             chx * T:(chx + 1) * T])
            op(queue, f, waits=waits, inc=sems[par], amt=16)
        return (sems, base)

    def emit_a_casts(chx, lb, g0, g1):
        sems, lbase = lb
        for g in range(g0, g1):
            par = g % 8
            w2 = [(sems[par], lbase[par] + 16 * (g // 8 + 1))]
            if chx == 0 and g == 0:
                w2.append((s_cst, 64))
            if chx > 0:
                if g == 0:
                    w2.append((s_peU, chx * 128))
                elif g == 8:
                    w2.append((s_peDn, chx * 32))
                elif g == 20:
                    w2.append((s_peDb2, chx))
            if chx == 1 and g == 0:
                w2.append((s_udc, 16 * NH))
            def fc(e, g=g, par=par):
                return e.activation(out=x0bf[:, g * T:(g + 1) * T],
                                    in_=x0Tbuf[par][:, :], func=AF.Copy)
            op("scalar", fc, waits=w2, inc=s_actA)

    b_w_base_prev = None
    a_ld_base_next = None
    for ch in range(NCH):
        ch_tok = ch * T

        if ch == 0:
            a_ld_base = emit_a_loads(0, "sync")
        else:
            a_ld_base = a_ld_base_next
        a_cast_base = 32 * ch
        if ch == 0:
            emit_a_casts(0, a_ld_base, 0, G)

        b_pe_base = s_peB.n
        b_peh_base = s_peBh.n
        b_act_base = s_actB.n
        s_wBc = s_wBg if ch == 0 else s_wB
        b_w_base = ([s_wBc[0].n, s_wBc[1].n, s_wBc[2].n] if ch == 0
                    else b_w_base_prev)
        b_st_base = [s_stB[0].n, s_stB[1].n, s_stB[2].n]
        for t in range(3):
            for g in range(G):
                idx = t * G + g
                for half in range(2):
                    hidx = idx * 2 + half
                    par = hidx % 3
                    waits = []
                    if hidx >= 3:
                        pj = hidx - 3
                        if pj % 2 == 0:
                            waits.append((s_peBh, b_peh_base + pj // 2 + 1))
                        else:
                            waits.append((s_peB, b_pe_base + pj // 2 + 1))
                    elif ch > 0:
                        waits.append((s_peB, b_pe_base))
                        if ch == 1:
                            waits.append((s_stB[par], 16 * 64))
                    if ch == 0:
                        if hidx >= 3:
                            waits.append((s_stB[par],
                                          b_st_base[par] + 16 * (hidx // 3)))
                        def fw(e, t=t, g=g, half=half, par=par):
                            return e.dma_start(
                                out=wbuf[par][:, :],
                                in_=d_wT[t, g, :, half * 2048:(half + 1) * 2048])
                        op("gpsimd", fw, waits=waits, inc=s_wBg[par], amt=16)
                        def fs(e, t=t, g=g, half=half, par=par):
                            return e.dma_start(
                                out=d_wst[t, g, :, half * 2048:(half + 1) * 2048],
                                in_=wbuf[par][:, :])
                        op("sync", fs,
                           waits=[(s_wBg[par], b_w_base[par] + 16 * (hidx // 3 + 1))],
                           inc=s_stB[par], amt=16)
                    elif hidx >= 3:
                        def fw(e, t=t, g=g, half=half, par=par):
                            return e.dma_start(
                                out=wbuf[par][:, :],
                                in_=d_wst[t, g, :, half * 2048:(half + 1) * 2048])
                        op("sync", fw, waits=waits, inc=s_wB[par], amt=16)
                pbank = pb[idx % 2]
                for s in range(G):
                    hidx = idx * 2 + s // 16
                    par = hidx % 3
                    waits = []
                    if s == 0:
                        if idx >= 2:
                            waits.append((s_actB, b_act_base + idx - 1))
                        elif ch > 0:
                            waits.append((s_actE, ch * 128))
                    if idx < 8:
                        waits.append((s_actA, a_cast_base + s + 1))
                    elif s == 0:
                        waits.append((s_actA, a_cast_base + G))
                    if s % 16 == 0:
                        waits.append((s_wBc[par],
                                      b_w_base[par] + 16 * (hidx // 3 + 1)))
                    def fm(e, s=s, par=par, pbank=pbank):
                        return e.matmul(
                            pbank[:, :],
                            wbuf[par][:, (s % 16) * 128:(s % 16 + 1) * 128],
                            x0bf[:, s * T:(s + 1) * T],
                            start=(s == 0), stop=(s == G - 1))
                    inc = s_peB if s == G - 1 else (s_peBh if s == 15 else None)
                    op("tensor", fm, waits=waits, inc=inc)
                plane = (qT, kT, vT)[t]
                waits = [(s_peB, b_pe_base + idx + 1)]
                if idx == 0 and ch == 0:
                    waits.append((s_cst, 64))
                if t == 0 and g == 0 and ch > 0:
                    waits.append((s_x1, ch * 64))
                if t == 2 and g == 0 and ch > 0:
                    waits.append((s_out[0], ch * 32))
                    waits.append((s_out[1], ch * 32))
                def fe(e, t=t, g=g, plane=plane, pbank=pbank):
                    return e.activation(
                        out=plane[:, g:g + (T - 1) * G + 1:G],
                        in_=pbank[:, :], func=AF.Identity,
                        bias=b_sb2[:, t * G + g:t * G + g + 1], scale=1.0)
                op("scalar", fe, waits=waits, inc=s_actB)

        e_wf_base_early = [s_wF[0].n, s_wF[1].n]
        e_wd_base_early = [s_wD[0].n, s_wD[1].n]
        for hh in (0, 1):
            waits_u = [(s_udc, 16 * NH), (s_peB, s_peB.n)]
            waits_d = [(s_udc, 16 * NH), (s_peB, s_peB.n)]
            if ch > 0:
                waits_u.append((s_peU, ch * 128))
                waits_d.append((s_peDn, ch * 32))
            def fwfu(e, hh=hh):
                return e.dma_start(out=ubuf[hh][:, :],
                                   in_=d_ust[hh, :, 0:2048])
            op("sync", fwfu, waits=waits_u, inc=s_wF[hh], amt=16)
            def fwfd(e, hh=hh):
                return e.dma_start(out=dbuf[hh][:, :],
                                   in_=d_ust[hh, :, 2048:4096])
            op("sync", fwfd, waits=waits_d, inc=s_wD[hh], amt=16)

        c_pea_base = s_peCa.n
        c_peb_base = s_peCb.n
        c_acta_base = s_actCa.n
        c_dvev_base = s_dvev.n
        c_dver_base = s_dver.n
        c_dvee_base = s_dvee.n
        c_dvem_base = s_dveCm.n

        for i in (0, 1):
            def fz(e, i=i):
                return e.memset(probs_bd[i][:, :], 0.0)
            op("vector", fz, waits=[(s_peB, b_pe_base + 96)] if i == 0 else (),
               inc=s_dveCm)

        for k in range(NSB + 2):
            if k < NSB:
                waits = [(s_actB, b_act_base + 96)]
                if k == 0 and ch == 0:
                    waits.append((s_cst, 64))
                if k >= 2:
                    waits.append((s_actCa, c_acta_base + k - 1))
                    waits.append((s_dvev, c_dvev_base + k - 1))
                for j2 in range(4):
                    tok0 = k * 16 + j2 * 4
                    def ft(e, k=k, j2=j2, tok0=tok0):
                        return e.transpose(
                            pv[k % 2][:, j2 * 128:(j2 + 1) * 128],
                            vT[:, tok0 * G:(tok0 + 4) * G],
                            idbf_sb[:, :])
                    op("tensor", ft, waits=waits if j2 == 0 else ())
                for j2 in range(4):
                    for j in range(4):
                        tok = k * 16 + j2 * 4 + j
                        def fsc(e, k=k, j2=j2, j=j, tok=tok):
                            return e.matmul(
                                ps[k % 2][j * 32:(j + 1) * 32,
                                          j2 * 32:(j2 + 1) * 32],
                                kT[:, tok * G:tok * G + G],
                                qT[:, tok * G:tok * G + G],
                                start=True, stop=True,
                                tile_position=(0, j * 32))
                        op("tensor", fsc)
                op("tensor", lambda e: None, inc=s_peCa, drain=True)
            m = k - 1
            if 0 <= m < NSB:
                waits = [(s_actCa, c_acta_base + m + 1)]
                if m >= 2:
                    waits.append((s_dver, c_dver_base + m - 1))
                else:
                    waits.append((s_dveE, ch * 32))
                if ch == 0 and m == 0:
                    waits.append((s_selfV, cst_v))
                def fsum(e, m=m):
                    return e.matmul(pb[2 + m % 2][:, :], onesK[:, :],
                                    probs_bd[m % 2][:, :], start=True, stop=True)
                op("tensor", fsum, waits=waits)
                for j2 in range(4):
                    waits2 = []
                    if j2 == 0:
                        waits2.append((s_dvee, c_dvee_base + max(m - 1, 0)))
                        waits2.append((s_dvev, c_dvev_base + m + 1))
                    def fctx(e, m=m, j2=j2):
                        return e.matmul(
                            pb[4 + m % 2][:, j2 * 128:(j2 + 1) * 128],
                            vtok_sb[m % 2][:, j2 * 128:(j2 + 1) * 128],
                            probs_bd[m % 2][:, j2 * 128:(j2 + 1) * 128],
                            start=True, stop=True)
                    op("tensor", fctx, waits=waits2)
            op("tensor", lambda e: None, inc=s_peCb, drain=True)

        if ch == 0:
            for h in range(NH):
                def fudc(e, h=h):
                    return e.dma_start(out=d_ust[h, :, :], in_=d_updn[h, :, :])
                op("gpsimd", fudc, inc=s_udc, amt=16)

        op("scalar", load_table(TBL_LNEXP))
        for k in range(NSB + 1):
            if k < NSB:
                waits = [(s_peCa, c_pea_base + k + 1)]
                if k < 2:
                    waits.append((s_dveCm, c_dvem_base + 2))
                for j in range(4):
                    def fe2(e, k=k, j=j):
                        pr_r = probs_bd[k % 2].rearrange("p (j2 c) -> p j2 c",
                                                         j2=4)
                        ps_r = ps[k % 2][:, 0:128].rearrange(
                            "p (j2 h) -> p j2 h", j2=4)
                        return e.activation(
                            out=pr_r[j * 32:(j + 1) * 32, :, j * 32:(j + 1) * 32],
                            in_=ps_r[j * 32:(j + 1) * 32, :, :],
                            func=AF.Exp, scale=RSQ)
                    op("scalar", fe2, waits=waits if j == 0 else (),
                       inc=s_actCa if j == 3 else None)
            m = k - 1
            if 0 <= m < NSB:
                waits = [(s_peCb, c_peb_base + m + 2),
                         (s_dvee, c_dvee_base + max(m - 1, 0))]
                def fln(e, m=m):
                    return e.activation(out=rb_sb[m % 2][:, :],
                                        in_=pb[2 + m % 2][:, :], func=AF.Ln)
                op("scalar", fln, waits=waits, inc=s_selfD)
                svl = s_selfD.n
                def fex(e, m=m):
                    return e.activation(out=rb_sb[m % 2][:, :],
                                        in_=rb_sb[m % 2][:, :], func=AF.Exp,
                                        scale=-1.0)
                op("scalar", fex, waits=[(s_selfD, svl)], inc=s_dver)
        for k in range(NSB + 2):
            if k < NSB:
                def fv(e, k=k):
                    return e.tensor_copy(out=vtok_sb[k % 2][:, :],
                                         in_=pv[k % 2][:, :])
                op("vector", fv, waits=[(s_peCa, c_pea_base + k + 1)],
                   inc=s_dvev)
            m2 = k - 2
            if 0 <= m2 < NSB:
                for j2 in range(4):
                    waits = [(s_peCb, c_peb_base + k)]
                    if j2 == 0:
                        waits.append((s_dver, c_dver_base + m2 + 1))
                    def fcm(e, m2=m2, j2=j2):
                        tok0 = m2 * 16 + j2 * 4
                        dst = sumT.rearrange("p (h tok) -> p tok h", h=G)[
                            :, tok0:tok0 + 4, :]
                        src = pb[4 + m2 % 2][:, j2 * 128:(j2 + 1) * 128].rearrange(
                            "p (j h) -> p j h", j=4)
                        rbs = rb_sb[m2 % 2][:, j2 * 128:(j2 + 1) * 128].rearrange(
                            "p (j h) -> p j h", j=4)
                        return e.tensor_tensor(out=dst, in0=rbs, in1=src,
                                               op=OP.mult)
                    last = (j2 == 3)
                    op("vector", fcm, waits=waits,
                       inc=s_dvee if last else None)
        d_pe_base = s_peD.n
        d_as_base = s_actDs.n
        d_da_base = s_dveDa.n
        d_dc_base = s_dveDc.n
        d_dn_base = s_dveDn.n
        d_ax_base = s_actDx.n
        d_xd_base = [x.n for x in s_xd]
        for g in range(G):
            par4 = g % 8
            par2 = g % 2
            waits = []
            if g >= 8:
                waits.append((s_dveDa, d_da_base + g - 7))
            if g < 8:
                waits.append((s_actA, a_cast_base + G))
                waits.append((s_peCb, c_peb_base + NSB + 2))
            def fxl(e, g=g, par4=par4, ch_tok=ch_tok):
                return e.dma_start(out=x0Tbuf[par4][:, :],
                                   in_=d_x0T[g * 128:(g + 1) * 128,
                                             ch_tok:ch_tok + T])
            op("sync", fxl, waits=waits, inc=s_xd[par4], amt=16)
            waits = [(s_xd[par4], d_xd_base[par4] + 16 * (g // 8 + 1))]
            if g == 0:
                waits.append((s_dvee, c_dvee_base + NSB))
            def fadd(e, g=g, par4=par4):
                return e.tensor_tensor(out=sumT[:, g * T:(g + 1) * T],
                                       in0=sumT[:, g * T:(g + 1) * T],
                                       in1=x0Tbuf[par4][:, :], op=OP.add)
            op("vector", fadd, waits=waits, inc=s_dveDa)
            sda = s_dveDa.n
            waits = [(s_dveDa, sda)]
            if g >= 2:
                waits.append((s_peD, d_pe_base + g - 1))
            def fcast(e, g=g, par2=par2):
                return e.tensor_copy(out=xbf_scr[par2][:, :],
                                     in_=sumT[:, g * T:(g + 1) * T])
            op("vector", fcast, waits=waits, inc=s_dveDc)
            waits = [(s_dveDa, sda)]
            if g >= 2:
                waits.append((s_peD, d_pe_base + g - 1))
            def fsq(e, g=g, par2=par2):
                return e.activation(out=sq_scr[par2][:, :],
                                    in_=sumT[:, g * T:(g + 1) * T],
                                    func=AF.Square)
            op("scalar", fsq, waits=waits, inc=s_actDs)
            waits = [(s_dveDc, d_dc_base + g + 1), (s_actDs, d_as_base + g + 1)]
            if g == 0:
                waits.append((s_dver, c_dver_base + NSB))
                if ch == 0:
                    waits.append((s_selfV, cst_v))
            def fm1(e, g=g, par2=par2):
                return e.matmul(pb[2][0:1, :], ones128[:, :], xbf_scr[par2][:, :],
                                start=(g == 0), stop=(g == G - 1))
            op("tensor", fm1, waits=waits)
            def fm2(e, g=g, par2=par2):
                return e.matmul(pb[3][0:1, :], ones128[:, :], sq_scr[par2][:, :],
                                start=(g == 0), stop=(g == G - 1))
            op("tensor", fm2, inc=s_peD)
        def fr1(e):
            return e.tensor_scalar(out=row_mean[:, :], in0=pb[2][0:1, :],
                                   scalar1=1.0 / 4096.0, scalar2=None, op0=OP.mult)
        op("vector", fr1, waits=[(s_peD, d_pe_base + G)], inc=s_selfV)
        def fr2(e):
            return e.tensor_scalar(out=row_var[:, :], in0=pb[3][0:1, :],
                                   scalar1=1.0 / 4096.0, scalar2=None, op0=OP.mult)
        op("vector", fr2, inc=s_selfV)
        sd2 = s_selfV.n
        def fr3(e):
            return e.tensor_tensor(out=row_A[:, :], in0=row_mean[:, :],
                                   in1=row_mean[:, :], op=OP.mult)
        op("vector", fr3, waits=[(s_selfV, sd2)], inc=s_selfV)
        sd3 = s_selfV.n
        def fr4(e):
            return e.tensor_tensor(out=row_var[:, :], in0=row_var[:, :],
                                   in1=row_A[:, :], op=OP.subtract)
        op("vector", fr4, waits=[(s_selfV, sd3)], inc=s_selfV)
        sd4 = s_selfV.n
        def fr5(e):
            return e.activation(out=row_B[:, :], in_=row_var[:, :], func=AF.Ln,
                                bias=eps_sb[0:1, 0:1], scale=1.0)
        op("scalar", fr5, waits=[(s_selfV, sd4)], inc=s_selfD)
        sd5 = s_selfD.n
        def fr6(e):
            return e.activation(out=row_A[:, :], in_=row_B[:, :], func=AF.Exp,
                                scale=-0.5)
        op("scalar", fr6, waits=[(s_selfD, sd5)], inc=s_selfD)
        sd6 = s_selfD.n
        def fr7(e):
            return e.tensor_tensor(out=row_B[:, :], in0=row_mean[:, :],
                                   in1=row_A[:, :], op=OP.mult)
        op("vector", fr7, waits=[(s_selfD, sd6)], inc=s_selfV)
        sd7 = s_selfV.n
        def fr8(e):
            return e.tensor_scalar(out=row_B[:, :], in0=row_B[:, :],
                                   scalar1=-1.0, scalar2=None, op0=OP.mult)
        op("vector", fr8, waits=[(s_selfV, sd7)], inc=s_selfV)
        sd8 = s_selfV.n
        def fhiA(e):
            return e.tensor_copy(out=row_hi[:, :], in_=row_A[:, :])
        op("vector", fhiA, waits=[(s_selfV, sd8)], inc=s_selfV)
        sd9 = s_selfV.n
        def floA(e):
            return e.tensor_tensor(out=row_lo[:, :], in0=row_A[:, :],
                                   in1=row_hi[:, :], op=OP.subtract)
        op("vector", floA, waits=[(s_selfV, sd9)], inc=s_selfV)
        sd10 = s_selfV.n
        def fbA1(e):
            return e.matmul(pb[0][:, :], ones1w[:, :], row_hi[:, :],
                            start=True, stop=False)
        op("tensor", fbA1, waits=[(s_selfV, sd10),
                                  (s_dvee, c_dvee_base + NSB)])
        def fbA2(e):
            return e.matmul(pb[0][:, :], ones1w[:, :], row_lo[:, :],
                            start=False, stop=True)
        op("tensor", fbA2, inc=s_peDb1, drain=True)
        db1 = s_peDb1.n
        def fhiB(e):
            return e.tensor_copy(out=row_hi[:, :], in_=row_B[:, :])
        op("vector", fhiB, waits=[(s_peDb1, db1)], inc=s_selfV)
        sd11 = s_selfV.n
        def floB(e):
            return e.tensor_tensor(out=row_lo[:, :], in0=row_B[:, :],
                                   in1=row_hi[:, :], op=OP.subtract)
        op("vector", floB, waits=[(s_selfV, sd11)], inc=s_selfV)
        sd12 = s_selfV.n
        def fbB1(e):
            return e.matmul(pb[1][:, :], ones1w[:, :], row_hi[:, :],
                            start=True, stop=False)
        op("tensor", fbB1, waits=[(s_selfV, sd12)])
        def fbB2(e):
            return e.matmul(pb[1][:, :], ones1w[:, :], row_lo[:, :],
                            start=False, stop=True)
        op("tensor", fbB2, inc=s_peDb2, drain=True)
        db2 = s_peDb2.n
        for g in range(G):
            waits = [(s_peDb2, db2)] if g == 0 else ()
            def fn1(e, g=g):
                return e.tensor_tensor(out=sumT[:, g * T:(g + 1) * T],
                                       in0=sumT[:, g * T:(g + 1) * T],
                                       in1=pb[0][:, :], op=OP.mult)
            op("vector", fn1, waits=waits, inc=s_selfV)
            sdg = s_selfV.n
            def fn2(e, g=g):
                return e.tensor_tensor(out=sumT[:, g * T:(g + 1) * T],
                                       in0=sumT[:, g * T:(g + 1) * T],
                                       in1=pb[1][:, :], op=OP.add)
            op("vector", fn2, waits=[(s_selfV, sdg)], inc=s_dveDn)
            def fxc(e, g=g):
                return e.activation(out=x1bf[:, g * T:(g + 1) * T],
                                    in_=sumT[:, g * T:(g + 1) * T],
                                    func=AF.Copy)
            op("scalar", fxc, waits=[(s_dveDn, d_dn_base + g + 1)], inc=s_actDx)
        d_x1v = d_x1bf.rearrange("(g p) tok -> p g tok", p=128)
        x1bf_v = x1bf.rearrange("p (g tok) -> p g tok", g=G)
        for piece in range(4):
            def fx1(e, piece=piece):
                return e.dma_start(
                    out=d_x1v[:, piece * 8:(piece + 1) * 8, :],
                    in_=x1bf_v[:, piece * 8:(piece + 1) * 8, :])
            op("sync", fx1, waits=[(s_actDx, d_ax_base + 8 * (piece + 1))],
               inc=s_x1, amt=16)

        s_wFc, s_wDc = s_wF, s_wD
        e_wf_base = e_wf_base_early
        e_wd_base = e_wd_base_early

        e_g_base = [x.n for x in s_g]
        e_peU_base = s_peU.n
        e_actE_base = s_actE.n
        e_peDn_base = s_peDn.n
        e_dveE_base = s_dveE.n
        e_sc_base = [x.n for x in s_sc]
        op("scalar", load_table(TBL_GELU))

        def emit_gathers(hh):
            for c2 in range(2):
                waits = [(s_x1, 16 * 4 * (ch + 1))]
                if hh == 0 and c2 == 0 and ch == 0:
                    waits.append((s_cst, 64))
                if hh >= 4:
                    waits.append((s_peU, e_peU_base + 8 * (hh - 3)))
                def fg(e, hh=hh, c2=c2):
                    return e.indirect_dma_start(
                        out=xg[hh % 4][:, c2 * T:(c2 + 1) * T], out_offset=None,
                        in_=d_x1bf[:, :],
                        in_offset=bass.IndirectOffsetOnAxis(
                            ap=perm_sb[:, 2 * hh + c2:2 * hh + c2 + 1], axis=0))
                op("gpsimd", fg, waits=waits, inc=s_g[hh % 4], amt=16)

        for hh in range(4):
            emit_gathers(hh)
        for h in range(NH):
            par = h % 2
            waits = [(s_udc, 16 * NH)]
            if h >= 2:
                waits.append((s_peU, e_peU_base + 8 * (h - 1)))
            else:
                waits.append((s_peB, b_pe_base + 96))
            if h >= 2:
                def fwfu(e, h=h, par=par):
                    return e.dma_start(out=ubuf[par][:, :], in_=d_ust[h, :, 0:2048])
                op("sync", fwfu, waits=waits, inc=s_wF[par], amt=16)
            waits = [(s_udc, 16 * NH)]
            if h >= 2:
                waits.append((s_peDn, e_peDn_base + 2 * (h - 1)))
            else:
                waits.append((s_peB, b_pe_base + 96))
            if h >= 2:
                def fwfd(e, h=h, par=par):
                    return e.dma_start(out=dbuf[par][:, :],
                                       in_=d_ust[h, :, 2048:4096])
                op("sync", fwfd, waits=waits, inc=s_wD[par], amt=16)
            if h + 4 < NH:
                emit_gathers(h + 4)
            for mt in range(8):
                ubank = (pb[0], pb[1], pb[4], pb[5])[mt % 4]
                waits = [(s_wFc[par], e_wf_base[par] + 16 * (h // 2 + 1)),
                         (s_g[h % 4], e_g_base[h % 4] + 32 * (h // 4 + 1))]
                if h * 8 + mt >= 4:
                    waits.append((s_actE, e_actE_base + h * 8 + mt - 3))
                elif mt < 2:
                    waits.append((s_dveDn, d_dn_base + G))
                else:
                    waits.append((s_dvee, c_dvee_base + NSB))
                for ct in range(2):
                    def fu(e, h=h, mt=mt, ct=ct, par=par, ubank=ubank):
                        return e.matmul(
                            ubank[:, :],
                            ubuf[par][:, ct * 1024 + mt * 128:
                                      ct * 1024 + (mt + 1) * 128],
                            xg[h % 4][:, ct * T:(ct + 1) * T],
                            start=(ct == 0), stop=(ct == 1))
                    op("tensor", fu, waits=waits if ct == 0 else (),
                       inc=s_peU if ct == 1 else None, drain=(ct == 1))
                def fgel(e, mt=mt, par=par, ubank=ubank):
                    return e.activation(out=gelu_sb[par][:, mt * T:(mt + 1) * T],
                                        in_=ubank[:, :], func=AF.Gelu)
                op("scalar", fgel, waits=[(s_peU, e_peU_base + h * 8 + mt + 1)],
                   inc=s_actE)
            for dt in range(2):
                dbank = pb[2 + dt]
                waits = [(s_wDc[par], e_wd_base[par] + 16 * (h // 2 + 1))]
                if h >= 1:
                    waits.append((s_dveE, e_dveE_base + 2 * (h - 1) + dt + 1))
                else:
                    waits.append((s_selfV, sd2))
                for mt in range(8):
                    w = list(waits) if mt == 0 else []
                    if dt == 0:
                        w.append((s_actE, e_actE_base + h * 8 + mt + 1))
                    def fd(e, dt=dt, mt=mt, par=par, dbank=dbank):
                        return e.matmul(
                            dbank[:, :],
                            dbuf[par][:, mt * 256 + dt * 128:
                                      mt * 256 + (dt + 1) * 128],
                            gelu_sb[par][:, mt * T:(mt + 1) * T],
                            start=(mt == 0), stop=(mt == 7))
                    op("tensor", fd, waits=w,
                       inc=s_peDn if mt == 7 else None, drain=(mt == 7))
                waits = [(s_peDn, e_peDn_base + 2 * h + dt + 1)]
                fb4 = (2 * h + dt) % 4
                if h >= 2:
                    waits.append((s_sc[fb4], e_sc_base[fb4] + 16 * (h // 2)))
                def fde(e, dt=dt, fb4=fb4, dbank=dbank):
                    return e.tensor_copy(out=ffn_st[fb4][:, :], in_=dbank[:, :])
                op("vector", fde, waits=waits, inc=s_dveE)
                def fsc2(e, h=h, dt=dt, fb4=fb4):
                    return e.indirect_dma_start(
                        out=d_ffnat[:, :],
                        out_offset=bass.IndirectOffsetOnAxis(
                            ap=perm_sb[:, 2 * h + dt:2 * h + dt + 1], axis=0),
                        in_=ffn_st[fb4][:, :], in_offset=None)
                op("gpsimd", fsc2,
                   waits=[(s_dveE, e_dveE_base + 2 * h + dt + 1)],
                   inc=s_sc[fb4], amt=16)

        b_w_base_prev = [s_wB[0].n, s_wB[1].n, s_wB[2].n]
        if ch + 1 < NCH:
            for hidx in range(3):
                par = hidx % 3
                t_n, g_n, half_n = hidx // 64, (hidx // 2) % G, hidx % 2
                waits = [(s_peB, 96 * (ch + 1))]
                if ch == 0:
                    waits.append((s_stB[par], 16 * 64))
                def fwe(e, t_n=t_n, g_n=g_n, half_n=half_n, par=par):
                    return e.dma_start(
                        out=wbuf[par][:, :],
                        in_=d_wst[t_n, g_n, :,
                                  half_n * 2048:(half_n + 1) * 2048])
                op("sync", fwe, waits=waits, inc=s_wB[par], amt=16)

        s6_fa_base = [x.n for x in s_fa]
        s6_p6_base = s_pe6.n
        s6_e_base = s_dve6e.n
        s6_n_base = s_act6n.n
        s6_out_base = [s_out[0].n, s_out[1].n]
        if ch + 1 < NCH:
            a_ld_base_next = emit_a_loads(ch + 1, "sync")
            emit_a_casts(ch + 1, a_ld_base_next, 0, G)
        op("scalar", load_table(TBL_SQRT))
        for k in range(G // 2):
            par = k % 3
            waits = [(s_sc[i], e_sc_base[i] + 16 * 8) for i in range(4)]
            if k >= 3:
                waits.append((s_fa[par], s6_fa_base[par] + 16 * (k // 3)))
            def ffb(e, k=k):
                return e.dma_start(
                    out=sumT.rearrange("p (g t) -> p g t", g=G)[
                        :, 2 * k:2 * k + 2, :],
                    in_=d_ffnat[256 * k:256 * (k + 1), :].rearrange(
                        "(b p) t -> p b t", p=128),
                    accum_op=OP.add)
            op("gpsimd", ffb, waits=waits, inc=s_fa[par], amt=16)
        for tt in range(4):
            for q in range(8):
                qidx = tt * 8 + q
                par = qidx % 2
                k0, k1 = 2 * q, 2 * q + 1
                waits = [(s_fa[k0 % 3], s6_fa_base[k0 % 3] + 16 * (k0 // 3 + 1)),
                         (s_fa[k1 % 3], s6_fa_base[k1 % 3] + 16 * (k1 // 3 + 1))]
                if qidx == 0:
                    waits.append((s_dveDn, d_dn_base + G))
                    waits.append((s_actE, e_actE_base + 8 * NH))
                    if ch == 0:
                        waits.append((s_cst, 64))
                if qidx >= 2:
                    waits.append((s_dve6e, s6_e_base + qidx - 1))
                for gg in range(4):
                    g = q * 4 + gg
                    def ftr(e, tt=tt, g=g, gg=gg, par=par):
                        return e.transpose(
                            pb[6 + par][:, gg * 128:(gg + 1) * 128],
                            sumT[:, g * T + tt * 128:g * T + (tt + 1) * 128],
                            id32_sb[:, :])
                    op("tensor", ftr, waits=waits if gg == 0 else ())
                op("tensor", lambda e: None, inc=s_pe6, drain=True)
                waits = [(s_pe6, s6_p6_base + qidx + 1)]
                if q == 0:
                    if tt >= 2:
                        waits.append((s_out[tt % 2],
                                      s6_out_base[tt % 2] + 16 * (tt // 2)))
                    else:
                        for i in range(4):
                            waits.append((s_sc[i], e_sc_base[i] + 16 * 8))
                def fev(e, tt=tt, q=q, par=par):
                    return e.tensor_copy(
                        out=out_st[tt % 2][:, q * 512:(q + 1) * 512],
                        in_=pb[6 + par][:, :])
                op("vector", fev, waits=waits, inc=s_dve6e)
            for jj in range(8):
                def fbn1(e, tt=tt, jj=jj):
                    return e.bn_stats(out=s6_bn[:, jj * 6:(jj + 1) * 6],
                                      in_=out_st[tt % 2][:, jj * 512:(jj + 1) * 512])
                op("vector", fbn1,
                   waits=[(s_dve6e, s6_e_base + (tt + 1) * 8)] if jj == 0 else (),
                   inc=s_self6)
            sb8 = s_self6.n
            def fagg(e, tt=tt):
                return e.bn_aggr(out=s6_mv[:, :], in_=s6_bn[:, :])
            op("vector", fagg, waits=[(s_self6, sb8)], inc=s_self6)
            sb9 = s_self6.n
            def fstd(e, tt=tt):
                return e.activation(out=s6_std[:, :], in_=s6_mv[:, 1:2],
                                    func=AF.Sqrt, bias=eps_sb[:, 0:1], scale=1.0)
            op("scalar", fstd, waits=[(s_self6, sb9)], inc=s_self6)
            sb10 = s_self6.n
            def frst(e, tt=tt):
                return e.reciprocal(out=s6_rstd[:, :], in_=s6_std[:, :])
            op("vector", frst, waits=[(s_self6, sb10)], inc=s_self6)
            sb11 = s_self6.n
            def fnm(e, tt=tt):
                return e.tensor_tensor(out=s6_nmr[:, :], in0=s6_mv[:, 0:1],
                                       in1=s6_rstd[:, :], op=OP.mult)
            op("vector", fnm, waits=[(s_self6, sb11)], inc=s_self6)
            sb12 = s_self6.n
            def fng(e, tt=tt):
                return e.tensor_scalar(out=s6_nmr[:, :], in0=s6_nmr[:, :],
                                       scalar1=-1.0, scalar2=None, op0=OP.mult)
            op("vector", fng, waits=[(s_self6, sb12)], inc=s_self6)
            sb13 = s_self6.n
            def fnorm(e, tt=tt):
                return e.activation(out=out_st[tt % 2][:, :],
                                    in_=out_st[tt % 2][:, :], func=AF.Identity,
                                    bias=s6_nmr[:, 0:1], scale=s6_rstd[:, 0:1])
            op("scalar", fnorm, waits=[(s_self6, sb13)], inc=s_act6n)
            def fout(e, tt=tt, ch=ch):
                return e.dma_start(
                    out=d_out[(ch * 4 + tt) * 128:(ch * 4 + tt + 1) * 128, :],
                    in_=out_st[tt % 2][:, :])
            op("sync", fout, waits=[(s_act6n, s6_n_base + tt + 1)],
               inc=s_out[tt % 2], amt=16)

    op("sync", lambda e: e.nop(),
       waits=[(s_out[0], s_out[0].n), (s_out[1], s_out[1].n)])

    blk = nc.Block().__enter__()

    import bass_rust

    def emit_engine(eng, name):
        evented = set()
        for o in OPL[name]:
            for (sem, val) in o.waits:
                eng.wait_ge(sem.h, val)
            instr = o.fn(eng)
            if o.incs:
                if instr is None:
                    raw = nc.cur_bb.bb.instructions[-1]
                    if raw.name in evented:
                        for (sem, amt, tgt) in o.incs:
                            eng.wait_ge(sem.h, tgt - amt)
                            eng.sem_inc(sem.h, amt)
                        continue
                    evented.add(raw.name)
                    instr = bass_rust.BassInstruction(raw)
                else:
                    try:
                        evented.add(instr.ins.name)
                    except Exception:
                        pass
                for (sem, amt, tgt) in o.incs:
                    instr.then_inc(sem.h, amt)

    @blk.sync
    def _(e):
        emit_engine(e, "sync")

    @blk.tensor
    def _(e):
        emit_engine(e, "tensor")

    @blk.scalar
    def _(e):
        emit_engine(e, "scalar")

    @blk.vector
    def _(e):
        emit_engine(e, "vector")

    @blk.gpsimd
    def _(e):
        emit_engine(e, "gpsimd")

    blk.__exit__(None, None, None)
    return nc


# version 27
# speedup vs baseline: 1.1240x; 1.0029x over previous
import numpy as np
import concourse.bass as bass
import concourse.mybir as mybir
from concourse.alu_op_type import AluOpType
from concourse.bass_utils import run_bass_kernel_spmd
from concourse.hw_specs import get_activation_tables

F32 = mybir.dt.float32
BF16 = mybir.dt.bfloat16
U32 = mybir.dt.uint32
AF = mybir.ActivationFunctionType
OP = AluOpType

NCORES = 8
NTOK = 2048
T = 512
NCH = NTOK // T
G = 32
NSB = T // 16
NH = 16
EPS = 1e-5
RSQ = float(1.0 / np.sqrt(128.0))


class Sem:
    def __init__(self, nc, name):
        self.h = nc.semaphore(name).__enter__()
        self.n = 0

    def bump(self, amt):
        self.n += amt
        return self.n


class Op:
    __slots__ = ("waits", "fn", "incs", "drain")

    def __init__(self, waits, fn, incs, drain):
        self.waits, self.fn, self.incs, self.drain = waits, fn, incs, drain


def kernel(L0_attn_hidden_states, L0_attn_wq, L0_attn_bq, L0_attn_wk, L0_attn_bk,
           L0_attn_wv, L0_attn_bv, L0_attn_wl, L0_attn_bl,
           L0_ffn_up, L0_ffn_down, L0_ffn_perm):
    import ml_dtypes
    x = np.ascontiguousarray(np.asarray(L0_attn_hidden_states, dtype=np.float32)
                             .reshape(NCORES * NTOK, 4096))

    def blockT(w):
        w4 = np.asarray(w, np.float32).reshape(G, 128, G, 128)
        return np.ascontiguousarray(w4.transpose(0, 3, 2, 1).reshape(G, 128, 4096))

    wT = np.ascontiguousarray(np.stack(
        [blockT(L0_attn_wq), blockT(L0_attn_wk), blockT(L0_attn_wv)]))
    bias = np.ascontiguousarray(
        np.stack([np.ascontiguousarray(np.asarray(b, np.float32).reshape(G, 128).T)
                  for b in (L0_attn_bq, L0_attn_bk, L0_attn_bv)]))
    up5 = np.asarray(L0_ffn_up, np.float32).reshape(NH, 2, 128, 1024) \
        .transpose(0, 2, 1, 3).reshape(NH, 128, 2048)
    dn5 = np.asarray(L0_ffn_down, np.float32).reshape(NH, 8, 128, 256) \
        .transpose(0, 2, 1, 3).reshape(NH, 128, 2048)
    updn = np.ascontiguousarray(np.concatenate([up5, dn5], axis=2))
    permT = np.ascontiguousarray(
        np.asarray(L0_ffn_perm).astype(np.int64).astype(np.uint32)
        .reshape(G, 128).T)
    id32 = np.eye(128, dtype=np.float32)
    idbf = np.eye(128, dtype=ml_dtypes.bfloat16)

    nc = build_program()

    in_maps = []
    for i in range(NCORES):
        xT = np.ascontiguousarray(x[i * NTOK:(i + 1) * NTOK].T)
        in_maps.append({"x0T": xT, "wT": wT, "bias": bias, "updn": updn,
                        "permT": permT, "id32": id32, "idbf": idbf})
    res = run_bass_kernel_spmd(nc, in_maps, core_ids=list(range(NCORES)))
    out = np.concatenate([r["out"] for r in res.results], axis=0)
    return out.reshape(2, 8192, 4096)


def build_program():
    nc = bass.Bass("TRN2", target_bir_lowering=False)
    OPL = {k: [] for k in ("sync", "tensor", "scalar", "vector", "gpsimd")}

    d_x0T = nc.dram_tensor("x0T", [4096, NTOK], F32, kind="ExternalInput")
    d_wT = nc.dram_tensor("wT", [3, G, 128, 4096], F32, kind="ExternalInput")
    d_bias = nc.dram_tensor("bias", [3, 128, G], F32, kind="ExternalInput")
    d_updn = nc.dram_tensor("updn", [NH, 128, 4096], F32, kind="ExternalInput")
    d_perm = nc.dram_tensor("permT", [128, G], U32, kind="ExternalInput")
    d_id32 = nc.dram_tensor("id32", [128, 128], F32, kind="ExternalInput")
    d_idbf = nc.dram_tensor("idbf", [128, 128], BF16, kind="ExternalInput")
    d_out = nc.dram_tensor("out", [NTOK, 4096], F32, kind="ExternalOutput")
    d_wst = nc.dram_tensor("wstash", [3, G, 128, 4096], BF16, kind="Internal")
    d_ust = nc.dram_tensor("updnstash", [NH, 128, 4096], BF16, kind="Internal")
    d_x1bf = nc.dram_tensor("x1bf", [4096, T], BF16, kind="Internal")
    d_ffnat = nc.dram_tensor("ffnat", [4096, T], F32, kind="Internal")

    arena_items = []
    arena_size = [0]

    def sb_at(off, name, shape, dtype):
        import functools, operator
        per_part = functools.reduce(operator.mul, shape[1:], 1) * mybir.dt.size(dtype)
        assert off % 32 == 0, name
        arena_size[0] = max(arena_size[0], off + per_part)
        h = nc.virtual_sbuf_tensor(name, shape, dtype)
        arena_items.append((h, off))
        return h

    sumT = sb_at(0, "sumT", [128, G * T], F32)
    qT = sb_at(65536, "qT", [128, G * T], BF16)
    kT = sb_at(98304, "kT", [128, G * T], BF16)
    x1bf = sb_at(65536, "x1bfp", [128, G * T], BF16)
    xbf_scr = [sb_at(98304 + i * 1024, f"xbfs{i}", [128, T], BF16) for i in (0, 1)]
    sq_scr = [sb_at(100352 + i * 1024, f"sqs{i}", [128, T], BF16) for i in (0, 1)]
    x0Tbuf = [sb_at(102400 + i * 2048, f"x0Tb{i}", [128, T], F32)
              for i in range(8)]
    vT = sb_at(131072, "vT", [128, G * T], BF16)
    gelu_sb = [sb_at(131072 + i * 8192, f"gelu{i}", [128, 8 * T], BF16)
               for i in (0, 1)]
    xg = [sb_at(147456 + i * 2048, f"xg{i}", [128, 2 * T], BF16)
          for i in (0, 1)] + [sb_at(155648 + i * 2048, f"xg{2+i}", [128, 2 * T],
                                    BF16) for i in (0, 1)]
    ffn_st = [sb_at(151552 + i * 2048, f"ffnst{i}", [128, T], F32)
              for i in (0, 1)] + [sb_at(159744 + i * 2048, f"ffnst{2+i}",
                                        [128, T], F32) for i in (0, 1)]
    out_st = [sb_at(131072 + i * 16384, f"outst{i}", [128, 4096], F32)
              for i in (0, 1)]
    x0bf = sb_at(163840, "x0bf", [128, G * T], BF16)
    ubuf = [sb_at(163840 + i * 4096, f"ubuf{i}", [128, 2048], BF16)
            for i in (0, 1)]
    dbuf = [sb_at(172032 + i * 4096, f"dbuf{i}", [128, 2048], BF16)
            for i in (0, 1)]
    fbuf = [sb_at(172032 + i * 4096, f"fbuf{i}", [128, 2 * T], F32)
            for i in range(3)]
    probs_bd = [sb_at(184320 + i * 1024, f"probsbd{i}", [128, 4 * 128], BF16)
                for i in (0, 1)]
    vtok_sb = [sb_at(186368 + i * 1024, f"vtok{i}", [128, 4 * 128], BF16)
               for i in (0, 1)]
    rb_sb = [sb_at(188416, "rb0", [128, T], F32),
             sb_at(190464, "rb1", [128, T], F32)]
    onesK = sb_at(210816, "onesK", [128, 128], BF16)
    row_mean = sb_at(184320, "rowmean", [1, T], F32)
    row_var = sb_at(186368, "rowvar", [1, T], F32)
    row_A = sb_at(188416, "rowA", [1, T], F32)
    row_B = sb_at(190464, "rowB", [1, T], F32)
    row_hi = sb_at(192512, "rowhi", [1, T], BF16)
    row_lo = sb_at(193536, "rowlo", [1, T], BF16)
    wbuf = [sb_at(196608 + i * 4096, f"wbuf{i}", [128, 2048], BF16)
            for i in (0, 1, 2)]
    b_sb2 = sb_at(208896, "b_sb2", [128, 3 * G], F32)
    perm_sb = sb_at(209280, "perm_sb", [128, G], U32)
    id32_sb = sb_at(209408, "id32_sb", [128, 128], F32)
    idbf_sb = sb_at(209920, "idbf_sb", [128, 128], BF16)
    ones128 = sb_at(210176, "ones128", [128, 1], BF16)
    ones1w = sb_at(210208, "ones1w", [1, 128], BF16)
    s6_std = sb_at(210464, "s6_std", [128, 1], F32)
    s6_rstd = sb_at(210496, "s6_rstd", [128, 1], F32)
    s6_nmr = sb_at(210528, "s6_nmr", [128, 1], F32)
    s6_bn = sb_at(210560, "s6_bn", [128, 48], F32)
    s6_mv = sb_at(210752, "s6_mv", [128, 2], F32)
    eps_sb = sb_at(210784, "eps_sb", [128, 1], F32)
    assert arena_size[0] <= 212800, arena_size[0]

    backing = nc.sbuf_tensor("arena", [128, arena_size[0]],
                             mybir.dt.uint8).__enter__()
    base_addr = nc.lookup_mloc(backing).addr
    for h, off in arena_items:
        nc._tensor(h.name, h.shape, h.dtype, type="SB")
        m = nc.lookup_mloc(h)
        m.allocated = True
        m.addr = base_addr + off

    pb = [nc.psum_tensor(f"pb{i}", [128, 512], F32).__enter__() for i in range(8)]
    pv = []
    for i in (0, 1):
        mb = nc.lookup_mloc(pb[6 + i])
        h1 = nc.virtual_psum_tensor(f"pv{i}", [128, 512], BF16)
        nc._tensor(h1.name, h1.shape, h1.dtype, type="PSUM")
        m = nc.lookup_mloc(h1)
        m.allocated = True; m.addr = mb.addr + 0; m.bank = mb.bank
        pv.append(h1)
    ps = [pb[0], pb[1]]

    S = lambda n: Sem(nc, n)
    s_cst = S("cst")
    s_ld = [S(f"ld{i}") for i in range(8)]
    s_ldg = [S(f"ldg{i}") for i in range(8)]
    s_wB = [S("wB0"), S("wB1"), S("wB2")]
    s_wBg = [S("wBg0"), S("wBg1"), S("wBg2")]
    s_stB = [S("stB0"), S("stB1"), S("stB2")]
    s_wF = [S("wF0"), S("wF1")]
    s_wFg = [S("wFg0"), S("wFg1")]
    s_wD = [S("wD0"), S("wD1")]
    s_udc = S("udc")
    s_stF = [S("stF0"), S("stF1")]
    s_g = [S("g0"), S("g1"), S("g2"), S("g3")]
    s_xd = [S(f"xd{i}") for i in range(8)]
    s_fb = [S(f"fb{i}") for i in range(3)]
    s_fa = [S(f"fa{i}") for i in range(3)]
    s_sc = [S("sc0"), S("sc1"), S("sc2"), S("sc3")]
    s_out = [S("out0"), S("out1")]
    s_x1 = S("x1")
    s_peB, s_peBh, s_actB, s_actA = S("peB"), S("peBh"), S("actB"), S("actA")
    s_peCa, s_peCb = S("peCa"), S("peCb")
    s_actCa, s_dvev, s_dveCm = S("actCa"), S("dvev"), S("dveCm")
    s_dver, s_dvee = S("dver"), S("dvee")
    s_peD, s_peDb1, s_peDb2 = S("peD"), S("peDb1"), S("peDb2")
    s_actDs, s_dveDa, s_dveDc, s_dveDn, s_actDx = \
        S("actDs"), S("dveDa"), S("dveDc"), S("dveDn"), S("actDx")
    s_peU, s_actE, s_peDn, s_dveE = S("peU"), S("actE"), S("peDn"), S("dveE")
    s_dve6, s_pe6, s_dve6e, s_act6n = S("dve6"), S("pe6"), S("dve6e"), S("act6n")
    s_selfD, s_selfV, s_self6 = S("selfD"), S("selfV"), S("self6")

    tables = list(get_activation_tables(nc.m.arch).keys())
    TBL_LNEXP = tables.index("natural_log_exp_and_others")
    TBL_GELU = tables.index("gelu_and_others")
    TBL_SQRT = tables.index("sqrt_and_others")

    def load_table(idx):
        def f(e):
            return e.add_instruction(mybir.InstLoadActFuncSet(
                name=nc.get_next_instruction_name(), act_func_set_id=idx,
                ins=[], outs=[]))
        return f

    def op(engine, fn, waits=(), inc=None, amt=1, drain=False):
        incs = []
        if inc is not None:
            inc.bump(amt)
            incs.append((inc, amt, inc.n))
        OPL[engine].append(Op([(s, v) for (s, v) in waits if v > 0],
                              fn, incs, drain))

    op("sync", lambda e: e.dma_start(
        out=b_sb2.rearrange("p (t g) -> p t g", t=3),
        in_=d_bias.rearrange("t p g -> p t g")),
        inc=s_cst, amt=16)
    op("sync", lambda e: e.dma_start(out=perm_sb[:, :], in_=d_perm[:, :]),
       inc=s_cst, amt=16)
    op("sync", lambda e: e.dma_start(out=id32_sb[:, :], in_=d_id32[:, :]),
       inc=s_cst, amt=16)
    op("sync", lambda e: e.dma_start(out=idbf_sb[:, :], in_=d_idbf[:, :]),
       inc=s_cst, amt=16)
    op("vector", lambda e: e.memset(ones128[:, :], 1.0), inc=s_selfV)
    op("vector", lambda e: e.memset(ones1w[:, :], 1.0), inc=s_selfV)
    op("vector", lambda e: e.memset(onesK[:, :], 1.0), inc=s_selfV)
    op("vector", lambda e: e.memset(eps_sb[:, :], EPS), inc=s_selfV)
    cst_v = s_selfV.n
    op("scalar", load_table(TBL_LNEXP))

    def emit_a_loads(chx, queue):
        sems = s_ld if queue == "sync" else s_ldg
        base = [x.n for x in sems]
        for g in range(G):
            par = g % 8
            waits = []
            if g >= 8:
                waits.append((s_actA, 32 * chx + g - 7))
            if chx > 0 and g < 8:
                waits.append((s_peCb, chx * (NSB + 2)))
                waits.append((s_dveDa, chx * G))
            def f(e, g=g, par=par, chx=chx):
                return e.dma_start(out=x0Tbuf[par][:, :],
                                   in_=d_x0T[g * 128:(g + 1) * 128,
                                             chx * T:(chx + 1) * T])
            op(queue, f, waits=waits, inc=sems[par], amt=16)
        return (sems, base)

    def emit_a_casts(chx, lb, g0, g1):
        sems, lbase = lb
        for g in range(g0, g1):
            par = g % 8
            w2 = [(sems[par], lbase[par] + 16 * (g // 8 + 1))]
            if chx == 0 and g == 0:
                w2.append((s_cst, 64))
            if chx > 0:
                if g == 0:
                    w2.append((s_peU, chx * 128))
                elif g == 8:
                    w2.append((s_peDn, chx * 32))
                elif g == 20:
                    w2.append((s_peDb2, chx))
            if chx == 1 and g == 0:
                w2.append((s_udc, 16 * NH))
            def fc(e, g=g, par=par):
                return e.activation(out=x0bf[:, g * T:(g + 1) * T],
                                    in_=x0Tbuf[par][:, :], func=AF.Copy)
            op("scalar", fc, waits=w2, inc=s_actA)

    b_w_base_prev = None
    a_ld_base_next = None
    for ch in range(NCH):
        ch_tok = ch * T

        if ch == 0:
            a_ld_base = emit_a_loads(0, "sync")
        else:
            a_ld_base = a_ld_base_next
        a_cast_base = 32 * ch
        if ch == 0:
            emit_a_casts(0, a_ld_base, 0, G)

        b_pe_base = s_peB.n
        b_peh_base = s_peBh.n
        b_act_base = s_actB.n
        s_wBc = s_wBg if ch == 0 else s_wB
        b_w_base = ([s_wBc[0].n, s_wBc[1].n, s_wBc[2].n] if ch == 0
                    else b_w_base_prev)
        b_st_base = [s_stB[0].n, s_stB[1].n, s_stB[2].n]
        for t in range(3):
            for g in range(G):
                idx = t * G + g
                for half in range(2):
                    hidx = idx * 2 + half
                    par = hidx % 3
                    waits = []
                    if hidx >= 3:
                        pj = hidx - 3
                        if pj % 2 == 0:
                            waits.append((s_peBh, b_peh_base + pj // 2 + 1))
                        else:
                            waits.append((s_peB, b_pe_base + pj // 2 + 1))
                    elif ch > 0:
                        waits.append((s_peB, b_pe_base))
                        if ch == 1:
                            waits.append((s_stB[par], 16 * 64))
                    if ch == 0:
                        if hidx >= 3:
                            waits.append((s_stB[par],
                                          b_st_base[par] + 16 * (hidx // 3)))
                        def fw(e, t=t, g=g, half=half, par=par):
                            return e.dma_start(
                                out=wbuf[par][:, :],
                                in_=d_wT[t, g, :, half * 2048:(half + 1) * 2048])
                        op("gpsimd", fw, waits=waits, inc=s_wBg[par], amt=16)
                        def fs(e, t=t, g=g, half=half, par=par):
                            return e.dma_start(
                                out=d_wst[t, g, :, half * 2048:(half + 1) * 2048],
                                in_=wbuf[par][:, :])
                        op("sync", fs,
                           waits=[(s_wBg[par], b_w_base[par] + 16 * (hidx // 3 + 1))],
                           inc=s_stB[par], amt=16)
                    elif hidx >= 3:
                        def fw(e, t=t, g=g, half=half, par=par):
                            return e.dma_start(
                                out=wbuf[par][:, :],
                                in_=d_wst[t, g, :, half * 2048:(half + 1) * 2048])
                        op("sync", fw, waits=waits, inc=s_wB[par], amt=16)
                pbank = pb[idx % 2]
                for s in range(G):
                    hidx = idx * 2 + s // 16
                    par = hidx % 3
                    waits = []
                    if s == 0:
                        if idx >= 2:
                            waits.append((s_actB, b_act_base + idx - 1))
                        elif ch > 0:
                            waits.append((s_actE, ch * 128))
                    if idx < 8 and ch == 0:
                        waits.append((s_actA, a_cast_base + s + 1))
                    elif s == 0:
                        waits.append((s_actA, a_cast_base + G))
                    if s % 16 == 0:
                        waits.append((s_wBc[par],
                                      b_w_base[par] + 16 * (hidx // 3 + 1)))
                    def fm(e, s=s, par=par, pbank=pbank):
                        return e.matmul(
                            pbank[:, :],
                            wbuf[par][:, (s % 16) * 128:(s % 16 + 1) * 128],
                            x0bf[:, s * T:(s + 1) * T],
                            start=(s == 0), stop=(s == G - 1))
                    inc = s_peB if s == G - 1 else (s_peBh if s == 15 else None)
                    op("tensor", fm, waits=waits, inc=inc)
                plane = (qT, kT, vT)[t]
                waits = [(s_peB, b_pe_base + idx + 1)]
                if idx == 0 and ch == 0:
                    waits.append((s_cst, 64))
                if t == 0 and g == 0 and ch > 0:
                    waits.append((s_x1, ch * 64))
                if t == 2 and g == 0 and ch > 0:
                    waits.append((s_out[0], ch * 32))
                    waits.append((s_out[1], ch * 32))
                def fe(e, t=t, g=g, plane=plane, pbank=pbank):
                    return e.activation(
                        out=plane[:, g:g + (T - 1) * G + 1:G],
                        in_=pbank[:, :], func=AF.Identity,
                        bias=b_sb2[:, t * G + g:t * G + g + 1], scale=1.0)
                op("scalar", fe, waits=waits, inc=s_actB)

        e_wf_base_early = [s_wF[0].n, s_wF[1].n]
        e_wd_base_early = [s_wD[0].n, s_wD[1].n]
        for hh in (0, 1):
            waits_u = [(s_udc, 16 * NH), (s_peB, s_peB.n)]
            waits_d = [(s_udc, 16 * NH), (s_peB, s_peB.n)]
            if ch > 0:
                waits_u.append((s_peU, ch * 128))
                waits_d.append((s_peDn, ch * 32))
            def fwfu(e, hh=hh):
                return e.dma_start(out=ubuf[hh][:, :],
                                   in_=d_ust[hh, :, 0:2048])
            op("sync", fwfu, waits=waits_u, inc=s_wF[hh], amt=16)
            def fwfd(e, hh=hh):
                return e.dma_start(out=dbuf[hh][:, :],
                                   in_=d_ust[hh, :, 2048:4096])
            op("sync", fwfd, waits=waits_d, inc=s_wD[hh], amt=16)

        c_pea_base = s_peCa.n
        c_peb_base = s_peCb.n
        c_acta_base = s_actCa.n
        c_dvev_base = s_dvev.n
        c_dver_base = s_dver.n
        c_dvee_base = s_dvee.n
        c_dvem_base = s_dveCm.n

        for i in (0, 1):
            def fz(e, i=i):
                return e.memset(probs_bd[i][:, :], 0.0)
            op("vector", fz, waits=[(s_peB, b_pe_base + 96)] if i == 0 else (),
               inc=s_dveCm)

        for k in range(NSB + 2):
            if k < NSB:
                waits = [(s_actB, b_act_base + 96)]
                if k == 0 and ch == 0:
                    waits.append((s_cst, 64))
                if k >= 2:
                    waits.append((s_actCa, c_acta_base + k - 1))
                    waits.append((s_dvev, c_dvev_base + k - 1))
                for j2 in range(4):
                    tok0 = k * 16 + j2 * 4
                    def ft(e, k=k, j2=j2, tok0=tok0):
                        return e.transpose(
                            pv[k % 2][:, j2 * 128:(j2 + 1) * 128],
                            vT[:, tok0 * G:(tok0 + 4) * G],
                            idbf_sb[:, :])
                    op("tensor", ft, waits=waits if j2 == 0 else ())
                for j2 in range(4):
                    for j in range(4):
                        tok = k * 16 + j2 * 4 + j
                        def fsc(e, k=k, j2=j2, j=j, tok=tok):
                            return e.matmul(
                                ps[k % 2][j * 32:(j + 1) * 32,
                                          j2 * 32:(j2 + 1) * 32],
                                kT[:, tok * G:tok * G + G],
                                qT[:, tok * G:tok * G + G],
                                start=True, stop=True,
                                tile_position=(0, j * 32))
                        op("tensor", fsc)
                op("tensor", lambda e: None, inc=s_peCa, drain=True)
            m = k - 1
            if 0 <= m < NSB:
                waits = [(s_actCa, c_acta_base + m + 1)]
                if m >= 2:
                    waits.append((s_dver, c_dver_base + m - 1))
                else:
                    waits.append((s_dveE, ch * 32))
                if ch == 0 and m == 0:
                    waits.append((s_selfV, cst_v))
                def fsum(e, m=m):
                    return e.matmul(pb[2 + m % 2][:, :], onesK[:, :],
                                    probs_bd[m % 2][:, :], start=True, stop=True)
                op("tensor", fsum, waits=waits)
                for j2 in range(4):
                    waits2 = []
                    if j2 == 0:
                        waits2.append((s_dvee, c_dvee_base + max(m - 1, 0)))
                        waits2.append((s_dvev, c_dvev_base + m + 1))
                    def fctx(e, m=m, j2=j2):
                        return e.matmul(
                            pb[4 + m % 2][:, j2 * 128:(j2 + 1) * 128],
                            vtok_sb[m % 2][:, j2 * 128:(j2 + 1) * 128],
                            probs_bd[m % 2][:, j2 * 128:(j2 + 1) * 128],
                            start=True, stop=True)
                    op("tensor", fctx, waits=waits2)
            op("tensor", lambda e: None, inc=s_peCb, drain=True)

        if ch == 0:
            for h in range(NH):
                def fudc(e, h=h):
                    return e.dma_start(out=d_ust[h, :, :], in_=d_updn[h, :, :])
                op("gpsimd", fudc, inc=s_udc, amt=16)

        op("scalar", load_table(TBL_LNEXP))
        for k in range(NSB + 1):
            if k < NSB:
                waits = [(s_peCa, c_pea_base + k + 1)]
                if k < 2:
                    waits.append((s_dveCm, c_dvem_base + 2))
                for j in range(4):
                    def fe2(e, k=k, j=j):
                        pr_r = probs_bd[k % 2].rearrange("p (j2 c) -> p j2 c",
                                                         j2=4)
                        ps_r = ps[k % 2][:, 0:128].rearrange(
                            "p (j2 h) -> p j2 h", j2=4)
                        return e.activation(
                            out=pr_r[j * 32:(j + 1) * 32, :, j * 32:(j + 1) * 32],
                            in_=ps_r[j * 32:(j + 1) * 32, :, :],
                            func=AF.Exp, scale=RSQ)
                    op("scalar", fe2, waits=waits if j == 0 else (),
                       inc=s_actCa if j == 3 else None)
            m = k - 1
            if 0 <= m < NSB:
                waits = [(s_peCb, c_peb_base + m + 2),
                         (s_dvee, c_dvee_base + max(m - 1, 0))]
                def fln(e, m=m):
                    return e.activation(out=rb_sb[m % 2][:, :],
                                        in_=pb[2 + m % 2][:, :], func=AF.Ln)
                op("scalar", fln, waits=waits, inc=s_selfD)
                svl = s_selfD.n
                def fex(e, m=m):
                    return e.activation(out=rb_sb[m % 2][:, :],
                                        in_=rb_sb[m % 2][:, :], func=AF.Exp,
                                        scale=-1.0)
                op("scalar", fex, waits=[(s_selfD, svl)], inc=s_dver)
        for k in range(NSB + 2):
            if k < NSB:
                def fv(e, k=k):
                    return e.tensor_copy(out=vtok_sb[k % 2][:, :],
                                         in_=pv[k % 2][:, :])
                op("vector", fv, waits=[(s_peCa, c_pea_base + k + 1)],
                   inc=s_dvev)
            m2 = k - 2
            if 0 <= m2 < NSB:
                for j2 in range(4):
                    waits = [(s_peCb, c_peb_base + k)]
                    if j2 == 0:
                        waits.append((s_dver, c_dver_base + m2 + 1))
                    def fcm(e, m2=m2, j2=j2):
                        tok0 = m2 * 16 + j2 * 4
                        dst = sumT.rearrange("p (h tok) -> p tok h", h=G)[
                            :, tok0:tok0 + 4, :]
                        src = pb[4 + m2 % 2][:, j2 * 128:(j2 + 1) * 128].rearrange(
                            "p (j h) -> p j h", j=4)
                        rbs = rb_sb[m2 % 2][:, j2 * 128:(j2 + 1) * 128].rearrange(
                            "p (j h) -> p j h", j=4)
                        return e.tensor_tensor(out=dst, in0=rbs, in1=src,
                                               op=OP.mult)
                    last = (j2 == 3)
                    op("vector", fcm, waits=waits,
                       inc=s_dvee if last else None)
        d_pe_base = s_peD.n
        d_as_base = s_actDs.n
        d_da_base = s_dveDa.n
        d_dc_base = s_dveDc.n
        d_dn_base = s_dveDn.n
        d_ax_base = s_actDx.n
        d_xd_base = [x.n for x in s_xd]
        for g in range(G):
            par4 = g % 8
            par2 = g % 2
            waits = []
            if g >= 8:
                waits.append((s_dveDa, d_da_base + g - 7))
            if g < 8:
                waits.append((s_actA, a_cast_base + G))
                waits.append((s_peCb, c_peb_base + NSB + 2))
            def fxl(e, g=g, par4=par4, ch_tok=ch_tok):
                return e.dma_start(out=x0Tbuf[par4][:, :],
                                   in_=d_x0T[g * 128:(g + 1) * 128,
                                             ch_tok:ch_tok + T])
            op("sync", fxl, waits=waits, inc=s_xd[par4], amt=16)
            waits = [(s_xd[par4], d_xd_base[par4] + 16 * (g // 8 + 1))]
            if g == 0:
                waits.append((s_dvee, c_dvee_base + NSB))
            def fadd(e, g=g, par4=par4):
                return e.tensor_tensor(out=sumT[:, g * T:(g + 1) * T],
                                       in0=sumT[:, g * T:(g + 1) * T],
                                       in1=x0Tbuf[par4][:, :], op=OP.add)
            op("vector", fadd, waits=waits, inc=s_dveDa)
            sda = s_dveDa.n
            waits = [(s_dveDa, sda)]
            if g >= 2:
                waits.append((s_peD, d_pe_base + g - 1))
            def fcast(e, g=g, par2=par2):
                return e.tensor_copy(out=xbf_scr[par2][:, :],
                                     in_=sumT[:, g * T:(g + 1) * T])
            op("vector", fcast, waits=waits, inc=s_dveDc)
            waits = [(s_dveDa, sda)]
            if g >= 2:
                waits.append((s_peD, d_pe_base + g - 1))
            def fsq(e, g=g, par2=par2):
                return e.activation(out=sq_scr[par2][:, :],
                                    in_=sumT[:, g * T:(g + 1) * T],
                                    func=AF.Square)
            op("scalar", fsq, waits=waits, inc=s_actDs)
            waits = [(s_dveDc, d_dc_base + g + 1), (s_actDs, d_as_base + g + 1)]
            if g == 0:
                waits.append((s_dver, c_dver_base + NSB))
                if ch == 0:
                    waits.append((s_selfV, cst_v))
            def fm1(e, g=g, par2=par2):
                return e.matmul(pb[2][0:1, :], ones128[:, :], xbf_scr[par2][:, :],
                                start=(g == 0), stop=(g == G - 1))
            op("tensor", fm1, waits=waits)
            def fm2(e, g=g, par2=par2):
                return e.matmul(pb[3][0:1, :], ones128[:, :], sq_scr[par2][:, :],
                                start=(g == 0), stop=(g == G - 1))
            op("tensor", fm2, inc=s_peD)
        def fr1(e):
            return e.tensor_scalar(out=row_mean[:, :], in0=pb[2][0:1, :],
                                   scalar1=1.0 / 4096.0, scalar2=None, op0=OP.mult)
        op("vector", fr1, waits=[(s_peD, d_pe_base + G)], inc=s_selfV)
        def fr2(e):
            return e.tensor_scalar(out=row_var[:, :], in0=pb[3][0:1, :],
                                   scalar1=1.0 / 4096.0, scalar2=None, op0=OP.mult)
        op("vector", fr2, inc=s_selfV)
        sd2 = s_selfV.n
        def fr3(e):
            return e.tensor_tensor(out=row_A[:, :], in0=row_mean[:, :],
                                   in1=row_mean[:, :], op=OP.mult)
        op("vector", fr3, waits=[(s_selfV, sd2)], inc=s_selfV)
        sd3 = s_selfV.n
        def fr4(e):
            return e.tensor_tensor(out=row_var[:, :], in0=row_var[:, :],
                                   in1=row_A[:, :], op=OP.subtract)
        op("vector", fr4, waits=[(s_selfV, sd3)], inc=s_selfV)
        sd4 = s_selfV.n
        def fr5(e):
            return e.activation(out=row_B[:, :], in_=row_var[:, :], func=AF.Ln,
                                bias=eps_sb[0:1, 0:1], scale=1.0)
        op("scalar", fr5, waits=[(s_selfV, sd4)], inc=s_selfD)
        sd5 = s_selfD.n
        def fr6(e):
            return e.activation(out=row_A[:, :], in_=row_B[:, :], func=AF.Exp,
                                scale=-0.5)
        op("scalar", fr6, waits=[(s_selfD, sd5)], inc=s_selfD)
        sd6 = s_selfD.n
        def fr7(e):
            return e.tensor_tensor(out=row_B[:, :], in0=row_mean[:, :],
                                   in1=row_A[:, :], op=OP.mult)
        op("vector", fr7, waits=[(s_selfD, sd6)], inc=s_selfV)
        sd7 = s_selfV.n
        def fr8(e):
            return e.tensor_scalar(out=row_B[:, :], in0=row_B[:, :],
                                   scalar1=-1.0, scalar2=None, op0=OP.mult)
        op("vector", fr8, waits=[(s_selfV, sd7)], inc=s_selfV)
        sd8 = s_selfV.n
        def fhiA(e):
            return e.tensor_copy(out=row_hi[:, :], in_=row_A[:, :])
        op("vector", fhiA, waits=[(s_selfV, sd8)], inc=s_selfV)
        sd9 = s_selfV.n
        def floA(e):
            return e.tensor_tensor(out=row_lo[:, :], in0=row_A[:, :],
                                   in1=row_hi[:, :], op=OP.subtract)
        op("vector", floA, waits=[(s_selfV, sd9)], inc=s_selfV)
        sd10 = s_selfV.n
        def fbA1(e):
            return e.matmul(pb[0][:, :], ones1w[:, :], row_hi[:, :],
                            start=True, stop=False)
        op("tensor", fbA1, waits=[(s_selfV, sd10),
                                  (s_dvee, c_dvee_base + NSB)])
        def fbA2(e):
            return e.matmul(pb[0][:, :], ones1w[:, :], row_lo[:, :],
                            start=False, stop=True)
        op("tensor", fbA2, inc=s_peDb1, drain=True)
        db1 = s_peDb1.n
        def fhiB(e):
            return e.tensor_copy(out=row_hi[:, :], in_=row_B[:, :])
        op("vector", fhiB, waits=[(s_peDb1, db1)], inc=s_selfV)
        sd11 = s_selfV.n
        def floB(e):
            return e.tensor_tensor(out=row_lo[:, :], in0=row_B[:, :],
                                   in1=row_hi[:, :], op=OP.subtract)
        op("vector", floB, waits=[(s_selfV, sd11)], inc=s_selfV)
        sd12 = s_selfV.n
        def fbB1(e):
            return e.matmul(pb[1][:, :], ones1w[:, :], row_hi[:, :],
                            start=True, stop=False)
        op("tensor", fbB1, waits=[(s_selfV, sd12)])
        def fbB2(e):
            return e.matmul(pb[1][:, :], ones1w[:, :], row_lo[:, :],
                            start=False, stop=True)
        op("tensor", fbB2, inc=s_peDb2, drain=True)
        db2 = s_peDb2.n
        for g in range(G):
            waits = [(s_peDb2, db2)] if g == 0 else ()
            def fn1(e, g=g):
                return e.tensor_tensor(out=sumT[:, g * T:(g + 1) * T],
                                       in0=sumT[:, g * T:(g + 1) * T],
                                       in1=pb[0][:, :], op=OP.mult)
            op("vector", fn1, waits=waits, inc=s_selfV)
            sdg = s_selfV.n
            def fn2(e, g=g):
                return e.tensor_tensor(out=sumT[:, g * T:(g + 1) * T],
                                       in0=sumT[:, g * T:(g + 1) * T],
                                       in1=pb[1][:, :], op=OP.add)
            op("vector", fn2, waits=[(s_selfV, sdg)], inc=s_dveDn)
            def fxc(e, g=g):
                return e.activation(out=x1bf[:, g * T:(g + 1) * T],
                                    in_=sumT[:, g * T:(g + 1) * T],
                                    func=AF.Copy)
            op("scalar", fxc, waits=[(s_dveDn, d_dn_base + g + 1)], inc=s_actDx)
        d_x1v = d_x1bf.rearrange("(g p) tok -> p g tok", p=128)
        x1bf_v = x1bf.rearrange("p (g tok) -> p g tok", g=G)
        for piece in range(4):
            def fx1(e, piece=piece):
                return e.dma_start(
                    out=d_x1v[:, piece * 8:(piece + 1) * 8, :],
                    in_=x1bf_v[:, piece * 8:(piece + 1) * 8, :])
            op("sync", fx1, waits=[(s_actDx, d_ax_base + 8 * (piece + 1))],
               inc=s_x1, amt=16)

        s_wFc, s_wDc = s_wF, s_wD
        e_wf_base = e_wf_base_early
        e_wd_base = e_wd_base_early

        e_g_base = [x.n for x in s_g]
        e_peU_base = s_peU.n
        e_actE_base = s_actE.n
        e_peDn_base = s_peDn.n
        e_dveE_base = s_dveE.n
        e_sc_base = [x.n for x in s_sc]
        op("scalar", load_table(TBL_GELU))

        def emit_gathers(hh):
            for c2 in range(2):
                waits = [(s_x1, 16 * 4 * (ch + 1))]
                if hh == 0 and c2 == 0 and ch == 0:
                    waits.append((s_cst, 64))
                if hh >= 4:
                    waits.append((s_peU, e_peU_base + 8 * (hh - 3)))
                def fg(e, hh=hh, c2=c2):
                    return e.indirect_dma_start(
                        out=xg[hh % 4][:, c2 * T:(c2 + 1) * T], out_offset=None,
                        in_=d_x1bf[:, :],
                        in_offset=bass.IndirectOffsetOnAxis(
                            ap=perm_sb[:, 2 * hh + c2:2 * hh + c2 + 1], axis=0))
                op("gpsimd", fg, waits=waits, inc=s_g[hh % 4], amt=16)

        for hh in range(4):
            emit_gathers(hh)
        for h in range(NH):
            par = h % 2
            waits = [(s_udc, 16 * NH)]
            if h >= 2:
                waits.append((s_peU, e_peU_base + 8 * (h - 1)))
            else:
                waits.append((s_peB, b_pe_base + 96))
            if h >= 2:
                def fwfu(e, h=h, par=par):
                    return e.dma_start(out=ubuf[par][:, :], in_=d_ust[h, :, 0:2048])
                op("sync", fwfu, waits=waits, inc=s_wF[par], amt=16)
            waits = [(s_udc, 16 * NH)]
            if h >= 2:
                waits.append((s_peDn, e_peDn_base + 2 * (h - 1)))
            else:
                waits.append((s_peB, b_pe_base + 96))
            if h >= 2:
                def fwfd(e, h=h, par=par):
                    return e.dma_start(out=dbuf[par][:, :],
                                       in_=d_ust[h, :, 2048:4096])
                op("sync", fwfd, waits=waits, inc=s_wD[par], amt=16)
            if h + 4 < NH:
                emit_gathers(h + 4)
            for mt in range(8):
                ubank = (pb[0], pb[1], pb[4], pb[5])[mt % 4]
                waits = [(s_wFc[par], e_wf_base[par] + 16 * (h // 2 + 1)),
                         (s_g[h % 4], e_g_base[h % 4] + 32 * (h // 4 + 1))]
                if h * 8 + mt >= 4:
                    waits.append((s_actE, e_actE_base + h * 8 + mt - 3))
                elif mt < 2:
                    waits.append((s_dveDn, d_dn_base + G))
                else:
                    waits.append((s_dvee, c_dvee_base + NSB))
                for ct in range(2):
                    def fu(e, h=h, mt=mt, ct=ct, par=par, ubank=ubank):
                        return e.matmul(
                            ubank[:, :],
                            ubuf[par][:, ct * 1024 + mt * 128:
                                      ct * 1024 + (mt + 1) * 128],
                            xg[h % 4][:, ct * T:(ct + 1) * T],
                            start=(ct == 0), stop=(ct == 1))
                    op("tensor", fu, waits=waits if ct == 0 else (),
                       inc=s_peU if ct == 1 else None, drain=(ct == 1))
                def fgel(e, mt=mt, par=par, ubank=ubank):
                    return e.activation(out=gelu_sb[par][:, mt * T:(mt + 1) * T],
                                        in_=ubank[:, :], func=AF.Gelu)
                op("scalar", fgel, waits=[(s_peU, e_peU_base + h * 8 + mt + 1)],
                   inc=s_actE)
            for dt in range(2):
                dbank = pb[2 + dt]
                waits = [(s_actE, e_actE_base + 8 * (h + 1)),
                         (s_wDc[par], e_wd_base[par] + 16 * (h // 2 + 1))]
                if h >= 1:
                    waits.append((s_dveE, e_dveE_base + 2 * (h - 1) + dt + 1))
                else:
                    waits.append((s_selfV, sd2))
                for mt in range(8):
                    def fd(e, dt=dt, mt=mt, par=par, dbank=dbank):
                        return e.matmul(
                            dbank[:, :],
                            dbuf[par][:, mt * 256 + dt * 128:
                                      mt * 256 + (dt + 1) * 128],
                            gelu_sb[par][:, mt * T:(mt + 1) * T],
                            start=(mt == 0), stop=(mt == 7))
                    op("tensor", fd, waits=waits if mt == 0 else (),
                       inc=s_peDn if mt == 7 else None, drain=(mt == 7))
                waits = [(s_peDn, e_peDn_base + 2 * h + dt + 1)]
                fb4 = (2 * h + dt) % 4
                if h >= 2:
                    waits.append((s_sc[fb4], e_sc_base[fb4] + 16 * (h // 2)))
                def fde(e, dt=dt, fb4=fb4, dbank=dbank):
                    return e.tensor_copy(out=ffn_st[fb4][:, :], in_=dbank[:, :])
                op("vector", fde, waits=waits, inc=s_dveE)
                def fsc2(e, h=h, dt=dt, fb4=fb4):
                    return e.indirect_dma_start(
                        out=d_ffnat[:, :],
                        out_offset=bass.IndirectOffsetOnAxis(
                            ap=perm_sb[:, 2 * h + dt:2 * h + dt + 1], axis=0),
                        in_=ffn_st[fb4][:, :], in_offset=None)
                op("gpsimd", fsc2,
                   waits=[(s_dveE, e_dveE_base + 2 * h + dt + 1)],
                   inc=s_sc[fb4], amt=16)

        b_w_base_prev = [s_wB[0].n, s_wB[1].n, s_wB[2].n]
        if ch + 1 < NCH:
            for hidx in range(3):
                par = hidx % 3
                t_n, g_n, half_n = hidx // 64, (hidx // 2) % G, hidx % 2
                waits = [(s_peB, 96 * (ch + 1))]
                if ch == 0:
                    waits.append((s_stB[par], 16 * 64))
                def fwe(e, t_n=t_n, g_n=g_n, half_n=half_n, par=par):
                    return e.dma_start(
                        out=wbuf[par][:, :],
                        in_=d_wst[t_n, g_n, :,
                                  half_n * 2048:(half_n + 1) * 2048])
                op("sync", fwe, waits=waits, inc=s_wB[par], amt=16)

        s6_fa_base = [x.n for x in s_fa]
        s6_p6_base = s_pe6.n
        s6_e_base = s_dve6e.n
        s6_n_base = s_act6n.n
        s6_out_base = [s_out[0].n, s_out[1].n]
        if ch + 1 < NCH:
            a_ld_base_next = emit_a_loads(ch + 1, "sync")
            emit_a_casts(ch + 1, a_ld_base_next, 0, G)
        op("scalar", load_table(TBL_SQRT))
        for k in range(G // 2):
            par = k % 3
            waits = [(s_sc[i], e_sc_base[i] + 16 * 8) for i in range(4)]
            if k >= 3:
                waits.append((s_fa[par], s6_fa_base[par] + 16 * (k // 3)))
            def ffb(e, k=k):
                return e.dma_start(
                    out=sumT.rearrange("p (g t) -> p g t", g=G)[
                        :, 2 * k:2 * k + 2, :],
                    in_=d_ffnat[256 * k:256 * (k + 1), :].rearrange(
                        "(b p) t -> p b t", p=128),
                    accum_op=OP.add)
            op("gpsimd", ffb, waits=waits, inc=s_fa[par], amt=16)
        for tt in range(4):
            for q in range(8):
                qidx = tt * 8 + q
                par = qidx % 2
                k0, k1 = 2 * q, 2 * q + 1
                waits = [(s_fa[k0 % 3], s6_fa_base[k0 % 3] + 16 * (k0 // 3 + 1)),
                         (s_fa[k1 % 3], s6_fa_base[k1 % 3] + 16 * (k1 // 3 + 1))]
                if qidx == 0:
                    waits.append((s_dveDn, d_dn_base + G))
                    waits.append((s_actE, e_actE_base + 8 * NH))
                    if ch == 0:
                        waits.append((s_cst, 64))
                if qidx >= 2:
                    waits.append((s_dve6e, s6_e_base + qidx - 1))
                for gg in range(4):
                    g = q * 4 + gg
                    def ftr(e, tt=tt, g=g, gg=gg, par=par):
                        return e.transpose(
                            pb[6 + par][:, gg * 128:(gg + 1) * 128],
                            sumT[:, g * T + tt * 128:g * T + (tt + 1) * 128],
                            id32_sb[:, :])
                    op("tensor", ftr, waits=waits if gg == 0 else ())
                op("tensor", lambda e: None, inc=s_pe6, drain=True)
                waits = [(s_pe6, s6_p6_base + qidx + 1)]
                if q == 0:
                    if tt >= 2:
                        waits.append((s_out[tt % 2],
                                      s6_out_base[tt % 2] + 16 * (tt // 2)))
                    else:
                        for i in range(4):
                            waits.append((s_sc[i], e_sc_base[i] + 16 * 8))
                def fev(e, tt=tt, q=q, par=par):
                    return e.tensor_copy(
                        out=out_st[tt % 2][:, q * 512:(q + 1) * 512],
                        in_=pb[6 + par][:, :])
                op("vector", fev, waits=waits, inc=s_dve6e)
            for jj in range(8):
                def fbn1(e, tt=tt, jj=jj):
                    return e.bn_stats(out=s6_bn[:, jj * 6:(jj + 1) * 6],
                                      in_=out_st[tt % 2][:, jj * 512:(jj + 1) * 512])
                op("vector", fbn1,
                   waits=[(s_dve6e, s6_e_base + (tt + 1) * 8)] if jj == 0 else (),
                   inc=s_self6)
            sb8 = s_self6.n
            def fagg(e, tt=tt):
                return e.bn_aggr(out=s6_mv[:, :], in_=s6_bn[:, :])
            op("vector", fagg, waits=[(s_self6, sb8)], inc=s_self6)
            sb9 = s_self6.n
            def fstd(e, tt=tt):
                return e.activation(out=s6_std[:, :], in_=s6_mv[:, 1:2],
                                    func=AF.Sqrt, bias=eps_sb[:, 0:1], scale=1.0)
            op("scalar", fstd, waits=[(s_self6, sb9)], inc=s_self6)
            sb10 = s_self6.n
            def frst(e, tt=tt):
                return e.reciprocal(out=s6_rstd[:, :], in_=s6_std[:, :])
            op("vector", frst, waits=[(s_self6, sb10)], inc=s_self6)
            sb11 = s_self6.n
            def fnm(e, tt=tt):
                return e.tensor_tensor(out=s6_nmr[:, :], in0=s6_mv[:, 0:1],
                                       in1=s6_rstd[:, :], op=OP.mult)
            op("vector", fnm, waits=[(s_self6, sb11)], inc=s_self6)
            sb12 = s_self6.n
            def fng(e, tt=tt):
                return e.tensor_scalar(out=s6_nmr[:, :], in0=s6_nmr[:, :],
                                       scalar1=-1.0, scalar2=None, op0=OP.mult)
            op("vector", fng, waits=[(s_self6, sb12)], inc=s_self6)
            sb13 = s_self6.n
            def fnorm(e, tt=tt):
                return e.activation(out=out_st[tt % 2][:, :],
                                    in_=out_st[tt % 2][:, :], func=AF.Identity,
                                    bias=s6_nmr[:, 0:1], scale=s6_rstd[:, 0:1])
            op("scalar", fnorm, waits=[(s_self6, sb13)], inc=s_act6n)
            def fout(e, tt=tt, ch=ch):
                return e.dma_start(
                    out=d_out[(ch * 4 + tt) * 128:(ch * 4 + tt + 1) * 128, :],
                    in_=out_st[tt % 2][:, :])
            op("sync", fout, waits=[(s_act6n, s6_n_base + tt + 1)],
               inc=s_out[tt % 2], amt=16)

    op("sync", lambda e: e.nop(),
       waits=[(s_out[0], s_out[0].n), (s_out[1], s_out[1].n)])

    blk = nc.Block().__enter__()

    import bass_rust

    def emit_engine(eng, name):
        evented = set()
        for o in OPL[name]:
            for (sem, val) in o.waits:
                eng.wait_ge(sem.h, val)
            instr = o.fn(eng)
            if o.incs:
                if instr is None:
                    raw = nc.cur_bb.bb.instructions[-1]
                    if raw.name in evented:
                        for (sem, amt, tgt) in o.incs:
                            eng.wait_ge(sem.h, tgt - amt)
                            eng.sem_inc(sem.h, amt)
                        continue
                    evented.add(raw.name)
                    instr = bass_rust.BassInstruction(raw)
                else:
                    try:
                        evented.add(instr.ins.name)
                    except Exception:
                        pass
                for (sem, amt, tgt) in o.incs:
                    instr.then_inc(sem.h, amt)

    @blk.sync
    def _(e):
        emit_engine(e, "sync")

    @blk.tensor
    def _(e):
        emit_engine(e, "tensor")

    @blk.scalar
    def _(e):
        emit_engine(e, "scalar")

    @blk.vector
    def _(e):
        emit_engine(e, "vector")

    @blk.gpsimd
    def _(e):
        emit_engine(e, "gpsimd")

    blk.__exit__(None, None, None)
    return nc
